# version 01
# baseline (speedup 1.0000x reference)
import numpy as np
import ml_dtypes
from contextlib import ExitStack

import concourse.bass as bass
import concourse.mybir as mybir
from concourse.bass_utils import run_bass_kernel_spmd

F32 = mybir.dt.float32
BF16 = mybir.dt.bfloat16
I32 = mybir.dt.int32
AF = mybir.ActivationFunctionType
ALU = mybir.AluOpType
AX = mybir.AxisListType

B, T, D = 2, 4096, 2048
H, HD = 16, 128
INNER = 8192
TOK = 1024
NT = TOK // 128
SCL = float(1.0 / np.sqrt(128.0))
N_W1 = float(D * INNER)
GROUP4 = [[0, 1, 2, 3], [4, 5, 6, 7]]
GROUP8 = [[0, 1, 2, 3, 4, 5, 6, 7]]

_DEBUG = ()
_BUILD_TO = 99
_RACECHECK = False
_S2_HEADS = 4
_S2_QGS = 8


class Ctr:
    def __init__(self, sem):
        self.sem = sem
        self.val = 0

    def inc(self, instr, n=1):
        instr.then_inc(self.sem, n)
        self.val += n
        return self.val


class Builder:
    def __init__(self):
        self.nc = bass.Bass("TRN2", target_bir_lowering=False,
                            detect_race_conditions=_RACECHECK)
        self.dbg = []
        nc = self.nc
        t = {}
        t["x"] = nc.dram_tensor("x", [TOK, D], F32, kind="ExternalInput")
        for n, shp in [("wq_sh", [256, D]), ("wk_sh", [256, D]),
                       ("wv_sh", [256, D]), ("wo_sh", [256, D]),
                       ("w1t_sh", [256, INNER]), ("w2t_sh", [1024, D]),
                       ("bias_pp", [128, 48]), ("bo_row", [1, D])]:
            t[n] = nc.dram_tensor(n, shp, F32, kind="ExternalInput")
        t["ident"] = nc.dram_tensor("ident", [128, 128], BF16, kind="ExternalInput")
        t["tri"] = nc.dram_tensor("tri", [128, 128], BF16, kind="ExternalInput")
        t["hoffs"] = nc.dram_tensor("hoffs", [1, 80], I32, kind="ExternalInput")
        t["out"] = nc.dram_tensor("out", [TOK, D], F32, kind="ExternalOutput")

        t["qkv_in"] = nc.dram_tensor("qkv_in", [6144, TOK], BF16)
        t["qkv_g"] = nc.dram_tensor("qkv_g", [4 * 6144, TOK], BF16)
        t["at_in"] = nc.dram_tensor("at_in", [512, T], BF16)
        t["at_g"] = nc.dram_tensor("at_g", [D, T], BF16)
        t["wat_in"] = nc.dram_tensor("wat_in", [1024, D], BF16)
        t["wat_g"] = nc.dram_tensor("wat_g", [8192, D], BF16)
        t["w1q_in"] = nc.dram_tensor("w1q_in", [256, INNER], BF16)
        t["w1q_g"] = nc.dram_tensor("w1q_g", [D, INNER], BF16)
        t["w2q_in"] = nc.dram_tensor("w2q_in", [1024, D], BF16)
        t["w2q_g"] = nc.dram_tensor("w2q_g", [INNER, D], BF16)
        t["wsum_in"] = nc.dram_tensor("wsum_in", [1, 2], F32)
        t["fence_in"] = nc.dram_tensor("fence_in", [1, 2], F32)
        t["fence_out"] = nc.dram_tensor("fence_out", [1, 2], F32)
        t["wsum_g"] = nc.dram_tensor("wsum_g", [1, 2], F32)
        t["x2_dram"] = nc.dram_tensor("x2_dram", [TOK, D], F32)
        t["hqt_dram"] = nc.dram_tensor("hqt_dram", [INNER // 128, 128, TOK], BF16)
        self.t = t

    def cc_wait_all(self, g):
        g.wait_ge(self.s_cc.sem, self.s_cc.val)

    def cc_fence(self, g):
        t, s = self.t, self
        self.cc_wait_all(g)
        s.s_cc.inc(g.collective_compute(
            "AllReduce", ALU.add, replica_groups=GROUP8,
            ins=[t["fence_in"].ap().opt()], outs=[t["fence_out"].ap().opt()]))

    def dinc(self, eng, ctr, n=1):
        ctr.inc(eng.drain(fusable=False), n)

    def dump(self, name, shape, dt, src_ap_fn):
        if name not in _DEBUG:
            return
        d = self.nc.dram_tensor("dbg_" + name, shape, dt, kind="ExternalOutput")
        s = self
        with self.nc.Block() as blk:
            @blk.sync
            def _(sy):
                sy.wait_ge(s.s_cc.sem, s.s_cc.val)
                sy.wait_ge(s.s_st.sem, s.s_st.val)
                sy.wait_ge(s.s_ld.sem, s.s_ld.val)
                for i in range(4):
                    sy.wait_ge(s.s_ss[i].sem, s.s_ss[i].val)
                s.s_st.inc(sy.dma_start(out=d.ap(), in_=src_ap_fn()), 16)
                sy.wait_ge(s.s_st.sem, s.s_st.val)
        self.dbg.append("dbg_" + name)

    def build(self):
        nc, t = self.nc, self.t
        with ExitStack() as es:
            def sbt(name, shape, dt=F32):
                return es.enter_context(nc.sbuf_tensor(name, shape, dt))
            s = self
            s.A_BIG = sbt("A_BIG", [128, 32768], BF16)
            s.A_A = sbt("A_A", [128, 16384], BF16)
            s.A_S = sbt("A_S", [128, 8192], F32)
            s.A_B = sbt("A_B", [128, 8192], BF16)
            s.A_W = sbt("A_W", [128, 16384], BF16)
            s.A_I = sbt("A_I", [128, 2048], I32)
            s.ident = sbt("ident_sb", [128, 128], BF16)
            s.tri = sbt("tri_sb", [128, 128], BF16)
            s.ones_b = sbt("ones_b", [128, 128], BF16)
            s.ones_f = sbt("ones_f", [128, 128], F32)
            s.biasq = sbt("biasq", [128, 48], F32)
            s.bo_sb = sbt("bo_sb", [1, D], F32)
            s.bo_bf = sbt("bo_bf", [1, D], BF16)
            s.offs = sbt("offs_sb", [1, 80], I32)
            s.stc = sbt("stc", [128, 64], F32)
            s.st1 = sbt("st1", [1, 512], F32)
            s.wscal = sbt("wscal", [128, 8], F32)
            s.rown = sbt("rown", [1, 64], F32)
            s.rowl = sbt("rowl", [1, 512], F32)
            s.eps = sbt("eps_sb", [128, 2], F32)
            s.negc = sbt("negc", [128, 1], F32)
            s.pb = [es.enter_context(nc.psum_tensor(f"pb{i}", [128, 512], F32))
                    for i in range(8)]
            for nm in ["s_ld", "s_pe", "s_ac", "s_dv", "s_pl", "s_cc",
                       "s_st", "s_w"]:
                setattr(s, nm, Ctr(es.enter_context(nc.semaphore(nm))))
            s.s_sl = [Ctr(es.enter_context(nc.semaphore(f"s_sl{i}")))
                      for i in range(4)]
            s.s_ss = [Ctr(es.enter_context(nc.semaphore(f"s_ss{i}")))
                      for i in range(4)]
            s.roff = [nc.alloc_register(mybir.EngineType.SP, f"roff{i}")
                      for i in range(4)]

            self.p0a_consts_wat()
            self.p0b_wsums()
            self.p0c_quant()
            if _BUILD_TO >= 1:
                self.p1_ln1()
                self.p1_qkv()
            if _BUILD_TO >= 2:
                self.s2_attention()
            if _BUILD_TO >= 3:
                self.s3a_oproj()
                self.s3b_ln23()
                self.s3c_ffn()
            self.epilogue()
        return nc

    def p0a_consts_wat(self):
        nc, t, s = self.nc, self.t, self
        shards = [t["wq_sh"], t["wk_sh"], t["wv_sh"], t["wo_sh"]]
        for half in range(2):
            with nc.Block() as blk:
                @blk.sync
                def _(sy, half=half):
                    if half == 0:
                        for src, dst in [(t["ident"], s.ident), (t["tri"], s.tri),
                                         (t["bias_pp"], s.biasq),
                                         (t["bo_row"], s.bo_sb),
                                         (t["hoffs"], s.offs)]:
                            s.s_ld.inc(sy.dma_start(out=dst[:], in_=src.ap()), 16)
                    for j in range(4):
                        k = half * 4 + j
                        wsh, rt = shards[k // 2], k % 2
                        s.s_ld.inc(sy.dma_start(
                            out=s.A_S[:, j * 2048:(j + 1) * 2048],
                            in_=wsh.ap()[rt * 128:(rt + 1) * 128, :]), 16)

            with nc.Block() as blk:
                dv0 = s.s_dv.val
                ld_after = s.s_ld.val

                st_before = s.s_st.val

                @blk.vector
                def _(v, half=half, ld_after=ld_after, st_before=st_before):
                    v.wait_ge(s.s_ld.sem, ld_after)
                    v.wait_ge(s.s_st.sem, st_before)
                    if half == 0:
                        v.memset(s.ones_b[:], 1.0)
                        v.memset(s.ones_f[:], 1.0)
                        v.memset(s.eps[:, 0:1], 1e-5)
                        v.memset(s.eps[:, 1:2], 1e-12)
                        v.tensor_copy(out=s.bo_bf[:], in_=s.bo_sb[:])
                    for j in range(4):
                        v.tensor_copy(
                            out=s.A_B[:, j * 2048:(j + 1) * 2048],
                            in_=s.A_S[:, j * 2048:(j + 1) * 2048])
                        s.dinc(v, s.s_dv)

                @blk.sync
                def _(sy, half=half, dv0=dv0):
                    for j in range(4):
                        k = half * 4 + j
                        sy.wait_ge(s.s_dv.sem, dv0 + j + 1)
                        s.s_st.inc(sy.dma_start(
                            out=t["wat_in"].ap()[k * 128:(k + 1) * 128, :],
                            in_=s.A_B[:, j * 2048:(j + 1) * 2048]), 16)

        if "marker" in _DEBUG:
            with nc.Block() as blk:
                @blk.vector
                def _(v):
                    v.memset(s.A_W[:, 0:1024], 3.0)
                    s.dinc(v, s.s_dv)

                @blk.sync
                def _(sy):
                    sy.wait_ge(s.s_dv.sem, s.s_dv.val)
                    for r in range(48):
                        s.s_st.inc(sy.dma_start(
                            out=t["qkv_in"].ap()[r * 128:(r + 1) * 128, :],
                            in_=s.A_W[:, 0:1024]), 16)
                    sy.wait_ge(s.s_st.sem, s.s_st.val)
        with nc.Block() as blk:
            @blk.sync
            def _(sy):
                s.s_st.inc(sy.dma_start(out=t["fence_in"].ap(),
                                        in_=s.eps[0:1, 0:2]), 16)

            @blk.gpsimd
            def _(g):
                g.wait_ge(s.s_st.sem, s.s_st.val)
                self.cc_wait_all(g)
                s.s_cc.inc(g.collective_compute(
                    "AllGather", ALU.bypass, replica_groups=GROUP8,
                    ins=[t["wat_in"].ap().opt()],
                    outs=[t["wat_g"].ap().opt()]))
                self.cc_fence(g)
        self.cc_wat = s.s_cc.val

    def p0b_wsums(self):
        nc, t, s = self.nc, self.t, self
        chunks = ([("w1t_sh", rt, cc) for rt in range(2) for cc in range(4)] +
                  [("w2t_sh", rt, 0) for rt in range(8)])
        nW1 = 8
        with nc.Block() as blk:
            ac0, pe0 = s.s_ac.val, s.s_pe.val
            sl0 = [c.val for c in s.s_sl]

            @blk.sync
            def _(sy):
                for k, (nm, rt, cc) in enumerate(chunks):
                    if k >= 2:
                        sy.wait_ge(s.s_ac.sem, ac0 + k - 1)
                    s.s_sl[k % 2].inc(sy.dma_start(
                        out=s.A_S[:, (k % 2) * 2048:(k % 2 + 1) * 2048],
                        in_=t[nm].ap()[rt * 128:(rt + 1) * 128,
                                       cc * 2048:(cc + 1) * 2048]), 16)

            @blk.scalar
            def _(sc):
                for k in range(len(chunks)):
                    sc.wait_ge(s.s_sl[k % 2].sem, sl0[k % 2] + 16 * (k // 2 + 1))
                    if k >= 2:
                        sc.wait_ge(s.s_pe.sem, pe0 + k - 1)
                    sc.activation(
                        out=s.A_S[:, 4096 + (k % 2) * 2048:
                                  4096 + (k % 2 + 1) * 2048],
                        in_=s.A_S[:, (k % 2) * 2048:(k % 2 + 1) * 2048],
                        func=AF.Abs)
                    s.dinc(sc, s.s_ac)

            @blk.tensor
            def _(te):
                for k in range(len(chunks)):
                    te.wait_ge(s.s_ac.sem, ac0 + k + 1)
                    bank = s.pb[0] if k < nW1 else s.pb[1]
                    first, last = k in (0, nW1), k in (nW1 - 1, len(chunks) - 1)
                    for q in range(4):
                        ins = te.matmul(
                            bank[0:1, :], s.ones_f[:, 0:1],
                            s.A_S[:, 4096 + (k % 2) * 2048 + q * 512:
                                  4096 + (k % 2) * 2048 + (q + 1) * 512],
                            start=(first and q == 0), stop=(last and q == 3))
                    s.dinc(te, s.s_pe)

        with nc.Block() as blk:
            @blk.vector
            def _(v):
                v.reduce_sum(out=s.st1[0:1, 0:1], in_=s.pb[0][0:1, :], axis=AX.X)
                v.reduce_sum(out=s.st1[0:1, 1:2], in_=s.pb[1][0:1, :], axis=AX.X)
                s.dinc(v, s.s_dv)

            @blk.sync
            def _(sy):
                sy.wait_ge(s.s_dv.sem, s.s_dv.val)
                s.s_st.inc(sy.dma_start(out=t["wsum_in"].ap(),
                                        in_=s.st1[0:1, 0:2]), 16)

            @blk.gpsimd
            def _(g):
                g.wait_ge(s.s_st.sem, s.s_st.val)
                self.cc_wait_all(g)
                s.s_cc.inc(g.collective_compute(
                    "AllReduce", ALU.add, replica_groups=GROUP8,
                    ins=[t["wsum_in"].ap().opt()],
                    outs=[t["wsum_g"].ap().opt()]))
                self.cc_fence(g)

            @blk.sync
            def _(sy):
                sy.wait_ge(s.s_cc.sem, s.s_cc.val)
                s.s_ld.inc(sy.dma_start(out=s.st1[0:1, 8:10],
                                        in_=t["wsum_g"].ap()), 16)

        with nc.Block() as blk:
            ld_now = s.s_ld.val

            @blk.vector
            def _(v, ld_now=ld_now):
                v.wait_ge(s.s_ld.sem, ld_now)
                v.tensor_scalar(out=s.st1[0:1, 12:13], in0=s.st1[0:1, 8:9],
                                scalar1=1.0 / N_W1, scalar2=1e-5,
                                op0=ALU.mult, op1=ALU.max)
                v.tensor_scalar(out=s.st1[0:1, 13:14], in0=s.st1[0:1, 9:10],
                                scalar1=1.0 / N_W1, scalar2=1e-5,
                                op0=ALU.mult, op1=ALU.max)
                v.drain()
                v.reciprocal(out=s.st1[0:1, 10:11], in_=s.st1[0:1, 12:13])
                v.reciprocal(out=s.st1[0:1, 11:12], in_=s.st1[0:1, 13:14])
                s.dinc(v, s.s_dv)

            @blk.tensor
            def _(te):
                te.wait_ge(s.s_dv.sem, s.s_dv.val)
                te.matmul(s.pb[2][:, 0:4], s.ones_f[0:1, :],
                          s.st1[0:1, 10:14], start=True, stop=True)
                s.dinc(te, s.s_pe)

            @blk.scalar
            def _(sc):
                sc.wait_ge(s.s_pe.sem, s.s_pe.val)
                s.s_ac.inc(sc.activation(out=s.wscal[:, 0:4],
                                         in_=s.pb[2][:, 0:4], func=AF.Copy))

    def p0c_quant(self):
        nc, t, s = self.nc, self.t, self
        chunks = ([("w1t_sh", "w1q_in", rt, cc, 0) for rt in range(2)
                   for cc in range(4)] +
                  [("w2t_sh", "w2q_in", rt, 0, 1) for rt in range(8)])
        with nc.Block() as blk:
            dv0, st0 = s.s_dv.val, s.s_st.val
            sl0 = [c.val for c in s.s_sl]
            ss0 = [c.val for c in s.s_ss]

            @blk.sync
            def _(sy):
                for k, (nm, _, rt, cc, _j) in enumerate(chunks):
                    if k >= 2:
                        sy.wait_ge(s.s_dv.sem, dv0 + 3 * (k - 2) + 1)
                    s.s_sl[k % 2].inc(sy.dma_start(
                        out=s.A_S[:, (k % 2) * 2048:(k % 2 + 1) * 2048],
                        in_=t[nm].ap()[rt * 128:(rt + 1) * 128,
                                       cc * 2048:(cc + 1) * 2048]), 16)

            @blk.vector
            def _(v):
                for k, (_, _, _, _, j) in enumerate(chunks):
                    v.wait_ge(s.s_sl[k % 2].sem, sl0[k % 2] + 16 * (k // 2 + 1))
                    v.wait_ge(s.s_st.sem, st0)
                    if k >= 2:
                        v.wait_ge(s.s_ss[k % 2].sem,
                                  ss0[k % 2] + 16 * ((k - 2) // 2 + 1))
                    src = s.A_S[:, (k % 2) * 2048:(k % 2 + 1) * 2048]
                    v.tensor_scalar(
                        out=s.A_S[:, 4096:6144], in0=src,
                        scalar1=s.wscal[:, j:j + 1], scalar2=None,
                        op0=ALU.mult)
                    s.dinc(v, s.s_dv)
                    v.tensor_copy(out=s.A_I[:], in_=s.A_S[:, 4096:6144])
                    v.drain()
                    v.tensor_scalar(
                        out=s.A_B[:, (k % 2) * 2048:(k % 2 + 1) * 2048],
                        in0=s.A_I[:], scalar1=1.0, scalar2=-1.0,
                        op0=ALU.min, op1=ALU.max)
                    s.dinc(v, s.s_dv, 2)

            @blk.scalar
            def _(sc):
                for k, (_, dst, rt, cc, _j) in enumerate(chunks):
                    sc.wait_ge(s.s_dv.sem, dv0 + 3 * (k + 1))
                    s.s_ss[k % 2].inc(sc.dma_start(
                        out=t[dst].ap()[rt * 128:(rt + 1) * 128,
                                        cc * 2048:(cc + 1) * 2048],
                        in_=s.A_B[:, (k % 2) * 2048:(k % 2 + 1) * 2048]), 16)

            @blk.gpsimd
            def _(g):
                for i in range(2):
                    g.wait_ge(s.s_ss[i].sem, s.s_ss[i].val)
                self.cc_wait_all(g)
                s.s_cc.inc(g.collective_compute(
                    "AllGather", ALU.bypass, replica_groups=GROUP8,
                    ins=[t["w1q_in"].ap().opt()],
                    outs=[t["w1q_g"].ap().opt()]))
                self.cc_wait_all(g)
                s.s_cc.inc(g.collective_compute(
                    "AllGather", ALU.bypass, replica_groups=GROUP8,
                    ins=[t["w2q_in"].ap().opt()],
                    outs=[t["w2q_g"].ap().opt()]))
                self.cc_fence(g)
        self.cc_wq = s.s_cc.val
        self.dump("wq1", [D, INNER], BF16, lambda: t["w1q_g"].ap())
        self.dump("wsum", [1, 16], F32, lambda: self.st1[0:1, 0:16])

    def p1_ln1(self):
        nc, t, s = self.nc, self.t, self
        with nc.Block() as blk:
            dv0, ac0, pe0 = s.s_dv.val, s.s_ac.val, s.s_pe.val
            sl0 = [c.val for c in s.s_sl]
            DV_T = 6
            AC_T = 2
            PE_T = 4

            @blk.sync
            def _(sy):
                for tt in range(NT):
                    if tt >= 2:
                        sy.wait_ge(s.s_ac.sem, ac0 + AC_T * (tt - 1))
                    s.s_sl[tt % 2].inc(sy.dma_start(
                        out=s.A_S[:, (tt % 2) * 2048:(tt % 2 + 1) * 2048],
                        in_=t["x"].ap()[tt * 128:(tt + 1) * 128, :]), 16)

            @blk.vector
            def _(v):
                for tt in range(NT):
                    xt = s.A_S[:, (tt % 2) * 2048:(tt % 2 + 1) * 2048]
                    v.wait_ge(s.s_sl[tt % 2].sem, sl0[tt % 2] + 16 * (tt // 2 + 1))
                    v.reduce_sum(out=s.stc[:, 0:1], in_=xt, axis=AX.X)
                    v.tensor_mul(s.A_S[:, 4096:6144], xt, xt)
                    v.drain()
                    v.reduce_sum(out=s.stc[:, 1:2], in_=s.A_S[:, 4096:6144],
                                 axis=AX.X)
                    v.tensor_scalar_mul(s.stc[:, 2:3], s.stc[:, 0:1], 1.0 / D)
                    v.drain()
                    v.tensor_scalar_mul(s.stc[:, 5:6], s.stc[:, 1:2], 1.0 / D)
                    v.tensor_mul(s.stc[:, 6:7], s.stc[:, 2:3], s.stc[:, 2:3])
                    v.drain()
                    v.tensor_sub(s.stc[:, 7:8], s.stc[:, 5:6], s.stc[:, 6:7])
                    s.dinc(v, s.s_dv)
                    v.wait_ge(s.s_ac.sem, ac0 + AC_T * tt + 1)
                    v.reciprocal(out=s.stc[:, 3:4], in_=s.stc[:, 9:10])
                    v.drain()
                    v.tensor_mul(s.stc[:, 8:9], s.stc[:, 2:3], s.stc[:, 3:4])
                    v.drain()
                    v.tensor_scalar_mul(s.stc[:, 4:5], s.stc[:, 8:9], -1.0)
                    s.dinc(v, s.s_dv)
                    for qb in range(4):
                        v.wait_ge(s.s_pe.sem, pe0 + PE_T * tt + qb + 1)
                        bank = s.pb[4 + qb % 2]
                        dst = bass.AP(s.A_A, qb * 4096 + tt * 128,
                                      [[16384, 128], [1024, 4], [1, 128]])
                        src = bass.AP(bank, 0,
                                      [[512, 128], [128, 4], [1, 128]])
                        v.tensor_copy(out=dst, in_=src)
                        s.dinc(v, s.s_dv)

            @blk.scalar
            def _(sc):
                for tt in range(NT):
                    xt = s.A_S[:, (tt % 2) * 2048:(tt % 2 + 1) * 2048]
                    sc.wait_ge(s.s_dv.sem, dv0 + DV_T * tt + 1)
                    sc.activation(out=s.stc[:, 9:10], in_=s.stc[:, 7:8],
                                  func=AF.Sqrt, bias=s.eps[:, 0:1])
                    s.dinc(sc, s.s_ac)
                    sc.wait_ge(s.s_dv.sem, dv0 + DV_T * tt + 2)
                    sc.activation(
                        out=s.A_B[:, (tt % 2) * 2048:(tt % 2 + 1) * 2048],
                        in_=xt, func=AF.Identity, scale=s.stc[:, 3:4],
                        bias=s.stc[:, 4:5])
                    s.dinc(sc, s.s_ac)

            @blk.tensor
            def _(te):
                for tt in range(NT):
                    te.wait_ge(s.s_ac.sem, ac0 + AC_T * tt + 2)
                    for qb in range(4):
                        bank = s.pb[4 + qb % 2]
                        if tt * 4 + qb >= 2:
                            pt, pq = divmod(tt * 4 + qb - 2, 4)
                            te.wait_ge(s.s_dv.sem,
                                       dv0 + DV_T * pt + 2 + pq + 1)
                        for j in range(4):
                            dch = qb * 4 + j
                            ins = te.matmul(
                                bank[:, j * 128:(j + 1) * 128],
                                s.A_B[:, (tt % 2) * 2048 + dch * 128:
                                      (tt % 2) * 2048 + (dch + 1) * 128],
                                s.ident[:], start=True, stop=True)
                        s.dinc(te, s.s_pe)
        self.ln1_dv_done = s.s_dv.val
        self.dump("xnT", [128, 16384], BF16, lambda: s.A_A[:, 0:16384])

    def p1_qkv(self):
        nc, t, s = self.nc, self.t, self
        with nc.Block() as blk:
            ld0, pe0, ac0 = s.s_ld.val, s.s_pe.val, s.s_ac.val
            ss0 = [c.val for c in s.s_ss]

            @blk.sync
            def _(sy):
                sy.wait_ge(s.s_cc.sem, self.cc_wat)
                for proj in range(3):
                    if proj > 0:
                        sy.wait_ge(s.s_pe.sem, pe0 + 32 * proj)
                    for d in range(16):
                        grow = (d // 2) * 1024 + proj * 256 + (d % 2) * 128
                        s.s_ld.inc(sy.dma_start(
                            out=s.A_BIG[:, d * 2048:(d + 1) * 2048],
                            in_=t["wat_g"].ap()[grow:grow + 128, :]), 16)
                    for oc in range(16):
                        go = proj * 16 + oc
                        sy.wait_ge(s.s_ac.sem, ac0 + 2 * go + 2)
                        row = oc * 384 + proj * 128
                        s.s_ss[go % 2].inc(sy.dma_start(
                            out=t["qkv_in"].ap()[row:row + 128, :],
                            in_=s.A_B[:, (go % 2) * 1024:
                                      (go % 2 + 1) * 1024]), 16)

            @blk.tensor
            def _(te):
                te.wait_ge(s.s_dv.sem, self.ln1_dv_done)
                g = 0
                for proj in range(3):
                    te.wait_ge(s.s_ld.sem, ld0 + 16 * 16 * (proj + 1))
                    for oc in range(16):
                        for th in range(2):
                            bank = s.pb[g % 4]
                            if g >= 4:
                                te.wait_ge(s.s_ac.sem, ac0 + g - 3)
                            for d in range(16):
                                ins = te.matmul(
                                    bank[:],
                                    s.A_BIG[:, d * 2048 + oc * 128:
                                            d * 2048 + (oc + 1) * 128],
                                    s.A_A[:, d * 1024 + th * 512:
                                          d * 1024 + (th + 1) * 512],
                                    start=(d == 0), stop=(d == 15))
                            s.dinc(te, s.s_pe)
                            g += 1

            @blk.scalar
            def _(sc):
                g = 0
                for proj in range(3):
                    for oc in range(16):
                        go = proj * 16 + oc
                        for th in range(2):
                            sc.wait_ge(s.s_pe.sem, pe0 + g + 1)
                            if th == 0 and go >= 2:
                                sc.wait_ge(s.s_ss[go % 2].sem,
                                           ss0[go % 2] + 16 * ((go - 2) // 2 + 1))
                            sc.activation(
                                out=s.A_B[:, (go % 2) * 1024 + th * 512:
                                          (go % 2) * 1024 + (th + 1) * 512],
                                in_=s.pb[g % 4][:], func=AF.Identity,
                                bias=s.biasq[:, proj * 16 + oc:
                                             proj * 16 + oc + 1])
                            s.dinc(sc, s.s_ac)
                            g += 1


            @blk.gpsimd
            def _(g_):
                for i in range(4):
                    g_.wait_ge(s.s_ss[i].sem, s.s_ss[i].val)
                self.cc_wait_all(g_)
                s.s_cc.inc(g_.collective_compute(
                    "AllGather", ALU.bypass, replica_groups=GROUP4,
                    ins=[t["qkv_in"].ap().opt()],
                    outs=[t["qkv_g"].ap().opt()]))
                self.cc_fence(g_)
        self.cc_qkv = s.s_cc.val
        self.dump("qkv", [6144, TOK], BF16, lambda: t["qkv_in"].ap())
        self.dump("qkvg", [4 * 6144, TOK], BF16, lambda: t["qkv_g"].ap())
        self.dump("qkv2", [6144, TOK], BF16, lambda: t["qkv_in"].ap())

    def s2_attention(self):
        nc, t, s = self.nc, self.t, self
        QO, KO, VO, VT, AT = 0, 4096, 8192, 24576, 28672
        for hl in range(_S2_HEADS):
            with nc.Block() as blk:
                pe0, ac0, dv0 = s.s_pe.val, s.s_ac.val, s.s_dv.val
                st0 = s.s_st.val
                qk0, v0 = s.s_sl[2].val, s.s_sl[3].val

                @blk.sync
                def _(sy, hl=hl):
                    if hl == 0:
                        sy.wait_ge(s.s_cc.sem, self.cc_qkv)
                    for p, base, sem in [(0, QO, s.s_sl[2]),
                                         (1, KO, s.s_sl[2]),
                                         (2, VO, s.s_sl[3])]:
                        idx = hl * 3 + p
                        if "statoff" in _DEBUG:
                            off = (hl * 384 + p * 128) * TOK
                        else:
                            sy.reg_load(s.roff[p], s.offs[0:1, idx:idx + 1])
                            off = s.roff[p]
                        src = bass.AP(t["qkv_g"], off,
                                      [[TOK, 128], [6144 * TOK, 4], [1, TOK]])
                        dst = bass.AP(s.A_BIG, base,
                                      [[32768, 128], [TOK, 4], [1, TOK]])
                        sem.inc(sy.dma_start(out=dst, in_=src,
                                             bounds_check=None), 16)

                @blk.scalar
                def _(sc, hl=hl):
                    sc.wait_ge(s.s_sl[2].sem, qk0 + 2 * 16)
                    sc.activation(out=s.A_S[:, 0:4096],
                                  in_=s.A_BIG[:, QO:QO + 4096], func=AF.Square)
                    s.dinc(sc, s.s_ac)
                    sc.activation(out=s.A_S[:, 4096:8192],
                                  in_=s.A_BIG[:, KO:KO + 4096], func=AF.Square)
                    s.dinc(sc, s.s_ac)
                    sc.wait_ge(s.s_pe.sem, pe0 + 11)
                    sc.activation(out=s.negc[:, 0:1], in_=s.pb[4][:, 0:1],
                                  func=AF.Copy)
                    s.dinc(sc, s.s_ac)
                    sc.drain()
                    pv_dinc = {}
                    pe_v = pe0 + 11
                    for qg in range(_S2_QGS):
                        nkv = 4 * qg + 4
                        for tt_ in range(nkv):
                            pe_v += 1
                            sc.wait_ge(s.s_pe.sem, pe_v)
                            if qg > 0:
                                sc.wait_ge(s.s_pe.sem, pv_dinc[qg - 1])
                            sc.activation(
                                out=s.A_A[:, tt_ * 512:(tt_ + 1) * 512],
                                in_=s.pb[tt_ % 2][:], func=AF.Exp,
                                scale=SCL, bias=s.negc[:, 0:1])
                            s.dinc(sc, s.s_ac)
                        pe_v += 1
                        pv_dinc[qg] = pe_v
                        pe_v += 1
                    self._s2_scalar_end = s.s_ac.val

                @blk.tensor
                def _(te, hl=hl):
                    te.wait_ge(s.s_ac.sem, ac0 + 1)
                    for ch in range(8):
                        te.matmul(s.pb[ch][0:1, :], s.ones_f[:, 0:1],
                                  s.A_S[:, ch * 512:(ch + 1) * 512],
                                  start=True, stop=True)
                    s.dinc(te, s.s_pe)
                    te.wait_ge(s.s_sl[3].sem, v0 + 1 * 16)
                    te.wait_ge(s.s_dv.sem, dv0 + 1)
                    for qb in range(8):
                        if qb >= 2:
                            te.wait_ge(s.s_dv.sem, dv0 + qb)
                        for j in range(4):
                            tt_ = qb * 4 + j
                            te.matmul(s.pb[4 + qb % 2][:, j * 128:(j + 1) * 128],
                                      s.A_BIG[:, VO + tt_ * 128:
                                              VO + (tt_ + 1) * 128],
                                      s.ident[:], start=True, stop=True)
                        s.dinc(te, s.s_pe)
                    te.wait_ge(s.s_ac.sem, ac0 + 2)
                    te.wait_ge(s.s_dv.sem, dv0 + 9)
                    for ch in range(8):
                        te.matmul(s.pb[ch][0:1, :], s.ones_f[:, 0:1],
                                  s.A_S[:, 4096 + ch * 512:
                                        4096 + (ch + 1) * 512],
                                  start=True, stop=True)
                    s.dinc(te, s.s_pe)
                    te.wait_ge(s.s_dv.sem, dv0 + 10)
                    te.matmul(s.pb[4][:, 0:1], s.ones_f[0:1, :],
                              s.st1[0:1, 30:31], start=True, stop=True)
                    s.dinc(te, s.s_pe)
                    ac_v = ac0 + 3
                    dv_v = dv0 + 10
                    evac_dinc = {}
                    for qg in range(_S2_QGS):
                        nkv = 4 * qg + 4
                        pvb = s.pb[2] if qg % 2 == 0 else s.pb[7]
                        lb = s.pb[3] if qg % 2 == 0 else s.pb[6]
                        for tt_ in range(nkv):
                            if tt_ >= 2:
                                te.wait_ge(s.s_ac.sem, ac_v + tt_ - 1)
                            elif qg > 0:
                                te.wait_ge(s.s_ac.sem, ac_v)
                            te.matmul(s.pb[tt_ % 2][:],
                                      s.A_BIG[:, KO + tt_ * 128:
                                              KO + (tt_ + 1) * 128],
                                      s.A_BIG[:, QO + qg * 512:
                                              QO + (qg + 1) * 512],
                                      start=True, stop=True)
                            s.dinc(te, s.s_pe)
                        if qg >= 2:
                            te.wait_ge(s.s_dv.sem, evac_dinc[qg - 2])
                        for tt_ in range(nkv):
                            te.wait_ge(s.s_ac.sem, ac_v + tt_ + 1)
                            if tt_ >= 4 * qg:
                                te.wait_ge(s.s_dv.sem,
                                           dv_v + (tt_ - 4 * qg) + 1)
                            te.matmul(pvb[:],
                                      s.A_BIG[:, VT + tt_ * 128:
                                              VT + (tt_ + 1) * 128],
                                      s.A_A[:, tt_ * 512:(tt_ + 1) * 512],
                                      start=(tt_ == 0), stop=(tt_ == nkv - 1))
                            te.matmul(lb[0:1, :], s.ones_b[:, 0:1],
                                      s.A_A[:, tt_ * 512:(tt_ + 1) * 512],
                                      start=(tt_ == 0), stop=(tt_ == nkv - 1))
                        s.dinc(te, s.s_pe)
                        te.wait_ge(s.s_dv.sem, dv_v + 4 + 1)
                        te.matmul(s.pb[4][:, 0:512], s.ones_f[0:1, :],
                                  s.rowl[0:1, :], start=True, stop=True)
                        s.dinc(te, s.s_pe)
                        ac_v += nkv
                        dv_v += 4 + 1 + 2
                        evac_dinc[qg] = dv_v

                @blk.vector
                def _(v, hl=hl):
                    v.wait_ge(s.s_st.sem, st0)
                    v.wait_ge(s.s_pe.sem, pe0 + 1)
                    for ch in range(8):
                        v.reduce_max(out=s.rown[0:1, ch:ch + 1],
                                     in_=s.pb[ch][0:1, :], axis=AX.X)
                    v.drain()
                    v.reduce_max(out=s.rown[0:1, 16:17],
                                 in_=s.rown[0:1, 0:8], axis=AX.X)
                    s.dinc(v, s.s_dv)
                    for qb in range(8):
                        v.wait_ge(s.s_pe.sem, pe0 + 2 + qb)
                        v.tensor_copy(
                            out=s.A_BIG[:, VT + qb * 512:VT + (qb + 1) * 512],
                            in_=s.pb[4 + qb % 2][:])
                        s.dinc(v, s.s_dv)
                    v.wait_ge(s.s_pe.sem, pe0 + 10)
                    for ch in range(8):
                        v.reduce_max(out=s.rown[0:1, 8 + ch:9 + ch],
                                     in_=s.pb[ch][0:1, :], axis=AX.X)
                    v.drain()
                    v.reduce_max(out=s.rown[0:1, 17:18],
                                 in_=s.rown[0:1, 8:16], axis=AX.X)
                    v.drain()
                    v.tensor_add(s.rown[0:1, 18:19], s.rown[0:1, 16:17],
                                 s.rown[0:1, 17:18])
                    v.drain()
                    v.tensor_scalar_mul(s.st1[0:1, 30:31],
                                        s.rown[0:1, 18:19], -0.5 * SCL)
                    s.dinc(v, s.s_dv)
                    ac_v = ac0 + 3
                    pe_v = pe0 + 11
                    dv_v = dv0 + 10
                    for qg in range(_S2_QGS):
                        nkv = 4 * qg + 4
                        pvb = s.pb[2] if qg % 2 == 0 else s.pb[7]
                        lb = s.pb[3] if qg % 2 == 0 else s.pb[6]
                        for j in range(4):
                            tt_ = 4 * qg + j
                            v.wait_ge(s.s_ac.sem, ac_v + tt_ + 1)
                            if j > 0:
                                v.memset(s.A_A[:, tt_ * 512:
                                               tt_ * 512 + j * 128], 0.0)
                            v.tensor_mul(
                                s.A_A[:, tt_ * 512 + j * 128:
                                      tt_ * 512 + (j + 1) * 128],
                                s.A_A[:, tt_ * 512 + j * 128:
                                      tt_ * 512 + (j + 1) * 128],
                                s.tri[:])
                            s.dinc(v, s.s_dv)
                        pe_v += nkv + 1
                        v.wait_ge(s.s_pe.sem, pe_v)
                        v.reciprocal(out=s.rowl[0:1, :], in_=lb[0:1, :])
                        s.dinc(v, s.s_dv)
                        pe_v += 1
                        v.wait_ge(s.s_pe.sem, pe_v)
                        v.tensor_copy(out=s.A_S[:, 4096:4608],
                                      in_=s.pb[4][:, 0:512])
                        s.dinc(v, s.s_dv)
                        v.tensor_mul(
                            s.A_BIG[:, AT + qg * 512:AT + (qg + 1) * 512],
                            pvb[:], s.A_S[:, 4096:4608])
                        s.dinc(v, s.s_dv)
                        ac_v += nkv
                        dv_v += 7

                @blk.sync
                def _(sy, hl=hl, dv0=dv0):
                    sy.wait_ge(s.s_dv.sem, dv0 + 10 + _S2_QGS * 7)
                    s.s_st.inc(sy.dma_start(
                        out=t["at_in"].ap()[hl * 128:(hl + 1) * 128, :],
                        in_=s.A_BIG[:, AT:AT + 4096]), 16)

        if _S2_HEADS == 4:
            with nc.Block() as blk:
                @blk.gpsimd
                def _(g):
                    g.wait_ge(s.s_st.sem, s.s_st.val)
                    self.cc_wait_all(g)
                    s.s_cc.inc(g.collective_compute(
                        "AllGather", ALU.bypass, replica_groups=GROUP4,
                        ins=[t["at_in"].ap().opt()],
                        outs=[t["at_g"].ap().opt()]))
                    self.cc_fence(g)
        self.cc_at = s.s_cc.val
        self.dump("atg", [D, T], BF16, lambda: t["at_g"].ap())
        self.dump("E", [128, 16384], BF16, lambda: s.A_A[:, 0:16384])
        self.dump("rowl", [1, 512], F32, lambda: s.rowl[0:1, :])
        self.dump("linvb", [128, 512], F32, lambda: s.A_S[:, 4096:4608])
        self.dump("negcd", [128, 1], F32, lambda: s.negc[:, 0:1])
        self.dump("atin", [512, T], BF16, lambda: t["at_in"].ap())
        self.dump("vt", [128, 4096], BF16, lambda: s.A_BIG[:, 24576:28672])
        self.dump("vraw", [128, 4096], BF16, lambda: s.A_BIG[:, 8192:12288])
        self.dump("qraw", [128, 4096], BF16, lambda: s.A_BIG[:, 0:4096])

    def s3a_oproj(self):
        nc, t, s = self.nc, self.t, self
        SD = float(np.sqrt(float(D)))
        AC_T, DV_T = 6, 7
        with nc.Block() as blk:
            ld0, pe0, ac0, dv0 = s.s_ld.val, s.s_pe.val, s.s_ac.val, s.s_dv.val
            sl0 = [c.val for c in s.s_sl]
            ss0 = [c.val for c in s.s_ss]

            @blk.sync
            def _(sy):
                sy.wait_ge(s.s_cc.sem, self.cc_at)
                sy.reg_load(s.roff[3], s.offs[0:1, 48:49])
                src = bass.AP(t["at_g"], s.roff[3],
                              [[T, 128], [128 * T, 16], [1, TOK]])
                dst = bass.AP(s.A_A, 0, [[16384, 128], [TOK, 16], [1, TOK]])
                s.s_ld.inc(sy.dma_start(out=dst, in_=src, bounds_check=None),
                           16)
                for d in range(16):
                    grow = (d // 2) * 1024 + 3 * 256 + (d % 2) * 128
                    s.s_ld.inc(sy.dma_start(
                        out=s.A_BIG[:, d * 2048:(d + 1) * 2048],
                        in_=t["wat_g"].ap()[grow:grow + 128, :]), 16)
                for tt in range(NT):
                    if tt >= 2:
                        sy.wait_ge(s.s_dv.sem, dv0 + DV_T * (tt - 2) + 3)
                    s.s_sl[tt % 2].inc(sy.dma_start(
                        out=s.A_S[:, 2048:4096],
                        in_=t["x"].ap()[tt * 128:(tt + 1) * 128, :]), 16)
                    sy.wait_ge(s.s_dv.sem, dv0 + DV_T * tt + 3)
                    s.s_ss[tt % 2].inc(sy.dma_start(
                        out=t["x2_dram"].ap()[tt * 128:(tt + 1) * 128, :],
                        in_=s.A_S[:, 4096:6144]), 16)

            @blk.tensor
            def _(te):
                te.wait_ge(s.s_ld.sem, ld0 + 17 * 16)
                for tt in range(NT):
                    bs = 0 if tt % 2 == 0 else 4
                    if tt >= 2:
                        te.wait_ge(s.s_ac.sem, ac0 + AC_T * (tt - 2) + 1)
                    for fc in range(16):
                        for dg in range(4):
                            te.matmul(
                                s.pb[bs + dg][:],
                                s.A_A[:, fc * 1024 + tt * 128:
                                      fc * 1024 + (tt + 1) * 128],
                                s.A_BIG[:, fc * 2048 + dg * 512:
                                        fc * 2048 + (dg + 1) * 512],
                                start=(fc == 0), stop=False)
                    for dg in range(4):
                        te.matmul(s.pb[bs + dg][:], s.ones_b[0:1, 0:128],
                                  s.bo_bf[0:1, dg * 512:(dg + 1) * 512],
                                  start=False, stop=True)
                    s.dinc(te, s.s_pe)

            @blk.scalar
            def _(sc):
                for tt in range(NT):
                    bs = 0 if tt % 2 == 0 else 4
                    sc.wait_ge(s.s_pe.sem, pe0 + tt + 1)
                    if tt >= 1:
                        sc.wait_ge(s.s_dv.sem, dv0 + DV_T * (tt - 1) + 7)
                    for dg in range(4):
                        sc.activation(out=s.A_S[:, dg * 512:(dg + 1) * 512],
                                      in_=s.pb[bs + dg][:], func=AF.Copy)
                    s.dinc(sc, s.s_ac)
                    sc.wait_ge(s.s_dv.sem, dv0 + DV_T * tt + 1)
                    sc.activation(out=s.stc[:, 9:10], in_=s.stc[:, 7:8],
                                  func=AF.Sqrt, bias=s.eps[:, 0:1])
                    s.dinc(sc, s.s_ac)
                    sc.wait_ge(s.s_dv.sem, dv0 + DV_T * tt + 2)
                    sc.activation(out=s.A_S[:, 6144:8192],
                                  in_=s.A_S[:, 0:2048], func=AF.Identity,
                                  scale=s.stc[:, 3:4], bias=s.stc[:, 4:5])
                    s.dinc(sc, s.s_ac)
                    sc.wait_ge(s.s_dv.sem, dv0 + DV_T * tt + 4)
                    sc.activation(out=s.stc[:, 19:20], in_=s.stc[:, 17:18],
                                  func=AF.Sqrt, bias=s.eps[:, 0:1])
                    s.dinc(sc, s.s_ac)
                    sc.wait_ge(s.s_dv.sem, dv0 + DV_T * tt + 5)
                    sc.activation(out=s.A_S[:, 6144:8192],
                                  in_=s.A_S[:, 4096:6144], func=AF.Identity,
                                  scale=s.stc[:, 13:14], bias=s.stc[:, 14:15])
                    sc.drain()
                    sc.activation(out=s.stc[:, 21:22], in_=s.stc[:, 20:21],
                                  func=AF.Sqrt)
                    s.dinc(sc, s.s_ac)
                    sc.wait_ge(s.s_dv.sem, dv0 + DV_T * tt + 6)
                    sc.activation(out=s.A_S[:, 0:2048],
                                  in_=s.A_S[:, 6144:8192], func=AF.Identity,
                                  scale=s.stc[:, 27:28])
                    s.dinc(sc, s.s_ac)

            @blk.vector
            def _(v):
                def ln_stats(src, varcol):
                    v.reduce_sum(out=s.stc[:, 0:1], in_=src, axis=AX.X)
                    v.tensor_mul(s.A_S[:, 6144:8192], src, src)
                    v.drain()
                    v.reduce_sum(out=s.stc[:, 1:2], in_=s.A_S[:, 6144:8192],
                                 axis=AX.X)
                    v.tensor_scalar_mul(s.stc[:, 2:3], s.stc[:, 0:1], 1.0 / D)
                    v.drain()
                    v.tensor_scalar_mul(s.stc[:, 5:6], s.stc[:, 1:2], 1.0 / D)
                    v.tensor_mul(s.stc[:, 6:7], s.stc[:, 2:3], s.stc[:, 2:3])
                    v.drain()
                    v.tensor_sub(s.stc[:, varcol:varcol + 1], s.stc[:, 5:6],
                                 s.stc[:, 6:7])

                for tt in range(NT):
                    v.wait_ge(s.s_ac.sem, ac0 + AC_T * tt + 1)
                    ln_stats(s.A_S[:, 0:2048], 7)
                    s.dinc(v, s.s_dv)
                    v.wait_ge(s.s_ac.sem, ac0 + AC_T * tt + 2)
                    v.reciprocal(out=s.stc[:, 3:4], in_=s.stc[:, 9:10])
                    v.drain()
                    v.tensor_mul(s.stc[:, 8:9], s.stc[:, 2:3], s.stc[:, 3:4])
                    v.drain()
                    v.tensor_scalar_mul(s.stc[:, 4:5], s.stc[:, 8:9], -1.0)
                    s.dinc(v, s.s_dv)
                    v.wait_ge(s.s_ac.sem, ac0 + AC_T * tt + 3)
                    v.wait_ge(s.s_sl[tt % 2].sem,
                              sl0[tt % 2] + 16 * (tt // 2 + 1))
                    if tt >= 1:
                        v.wait_ge(s.s_ss[(tt - 1) % 2].sem,
                                  ss0[(tt - 1) % 2]
                                  + 16 * ((tt - 1) // 2 + 1))
                    v.tensor_add(s.A_S[:, 4096:6144], s.A_S[:, 6144:8192],
                                 s.A_S[:, 2048:4096])
                    s.dinc(v, s.s_dv)
                    ln_stats(s.A_S[:, 4096:6144], 17)
                    s.dinc(v, s.s_dv)
                    v.wait_ge(s.s_ac.sem, ac0 + AC_T * tt + 4)
                    v.reciprocal(out=s.stc[:, 13:14], in_=s.stc[:, 19:20])
                    v.drain()
                    v.tensor_mul(s.stc[:, 8:9], s.stc[:, 2:3],
                                 s.stc[:, 13:14])
                    v.tensor_mul(s.stc[:, 15:16], s.stc[:, 17:18],
                                 s.stc[:, 13:14])
                    v.drain()
                    v.tensor_scalar_mul(s.stc[:, 14:15], s.stc[:, 8:9], -1.0)
                    v.tensor_mul(s.stc[:, 16:17], s.stc[:, 15:16],
                                 s.stc[:, 13:14])
                    v.drain()
                    v.tensor_scalar_mul(s.stc[:, 20:21], s.stc[:, 16:17],
                                        float(D))
                    s.dinc(v, s.s_dv)
                    v.wait_ge(s.s_ac.sem, ac0 + AC_T * tt + 5)
                    v.reduce_max(out=s.stc[:, 22:23], in_=s.A_S[:, 6144:8192],
                                 axis=AX.X, apply_absolute_value=True)
                    v.tensor_scalar_max(s.stc[:, 23:24], s.stc[:, 21:22],
                                        1e-12)
                    v.drain()
                    v.reciprocal(out=s.stc[:, 24:25], in_=s.stc[:, 23:24])
                    v.drain()
                    v.tensor_scalar_mul(s.stc[:, 25:26], s.stc[:, 24:25], SD)
                    v.drain()
                    v.tensor_mul(s.stc[:, 26:27], s.stc[:, 22:23],
                                 s.stc[:, 25:26])
                    v.drain()
                    v.tensor_scalar_max(s.stc[:, 28:29], s.stc[:, 26:27],
                                        1e-5)
                    v.drain()
                    v.reciprocal(out=s.stc[:, 29:30], in_=s.stc[:, 28:29])
                    v.drain()
                    v.tensor_scalar_mul(s.stc[:, 30:31], s.stc[:, 29:30],
                                        127.0)
                    v.drain()
                    v.tensor_mul(s.stc[:, 27:28], s.stc[:, 25:26],
                                 s.stc[:, 30:31])
                    v.tensor_scalar_mul(s.stc[:, 31:32], s.stc[:, 28:29],
                                        1.0 / 127.0)
                    s.dinc(v, s.s_dv)
                    v.wait_ge(s.s_ac.sem, ac0 + AC_T * tt + 6)
                    v.tensor_copy(out=s.A_I[:], in_=s.A_S[:, 0:2048])
                    v.drain()
                    v.tensor_scalar(out=s.A_W[:, tt * 2048:(tt + 1) * 2048],
                                    in0=s.A_I[:], scalar1=s.stc[:, 31:32],
                                    scalar2=None, op0=ALU.mult)
                    s.dinc(v, s.s_dv)
        self.dump("x2", [TOK, D], F32, lambda: t["x2_dram"].ap())
        self.dump("xqi", [128, 16384], BF16, lambda: s.A_W[:, 0:16384])

    def s3b_ln23(self):
        nc, t, s = self.nc, self.t, self
        with nc.Block() as blk:
            pe0, dv0 = s.s_pe.val, s.s_dv.val

            @blk.tensor
            def _(te):
                for tt in range(NT):
                    for qb in range(4):
                        b = tt * 4 + qb
                        if b >= 2:
                            te.wait_ge(s.s_dv.sem, dv0 + b - 1)
                        for j in range(4):
                            dch = qb * 4 + j
                            te.matmul(
                                s.pb[4 + b % 2][:, j * 128:(j + 1) * 128],
                                s.A_W[:, tt * 2048 + dch * 128:
                                      tt * 2048 + (dch + 1) * 128],
                                s.ident[:], start=True, stop=True)
                        s.dinc(te, s.s_pe)

            @blk.vector
            def _(v):
                for tt in range(NT):
                    for qb in range(4):
                        b = tt * 4 + qb
                        v.wait_ge(s.s_pe.sem, pe0 + b + 1)
                        dst = bass.AP(s.A_A, qb * 4096 + tt * 128,
                                      [[16384, 128], [1024, 4], [1, 128]])
                        src = bass.AP(s.pb[4 + b % 2], 0,
                                      [[512, 128], [128, 4], [1, 128]])
                        v.tensor_copy(out=dst, in_=src)
                        s.dinc(v, s.s_dv)
        self.dump("xqT", [128, 16384], BF16, lambda: s.A_A[:, 0:16384])

    def s3c_ffn(self):
        nc, t, s = self.nc, self.t, self
        SDI = float(np.sqrt(float(INNER)))
        for quad in range(2):
            tts = [quad * 4 + i for i in range(4)]
            with nc.Block() as blk:
                pe0, ac0 = s.s_pe.val, s.s_ac.val
                sl0 = [c.val for c in s.s_sl]

                @blk.sync
                def _(sy, quad=quad):
                    if quad == 0:
                        sy.wait_ge(s.s_cc.sem, self.cc_wq)
                    for ig in range(16):
                        if ig >= 2:
                            sy.wait_ge(s.s_pe.sem, pe0 + 4 * (ig - 1))
                        for d in range(16):
                            s.s_sl[2 + ig % 2].inc(sy.dma_start(
                                out=s.A_W[:, (ig % 2) * 8192 + d * 512:
                                          (ig % 2) * 8192 + (d + 1) * 512],
                                in_=t["w1q_g"].ap()[d * 128:(d + 1) * 128,
                                                    ig * 512:(ig + 1) * 512]),
                                16)

                @blk.tensor
                def _(te, quad=quad):
                    for ig in range(16):
                        te.wait_ge(s.s_sl[2 + ig % 2].sem,
                                   sl0[2 + ig % 2] + 256 * (ig // 2 + 1))
                        for i, tt in enumerate(tts):
                            if ig >= 1:
                                te.wait_ge(s.s_ac.sem,
                                           ac0 + 4 * (ig - 1) + i + 1)
                            for d in range(16):
                                te.matmul(
                                    s.pb[i][:],
                                    s.A_A[:, (d * 8 + tt) * 128:
                                          (d * 8 + tt + 1) * 128],
                                    s.A_W[:, (ig % 2) * 8192 + d * 512:
                                          (ig % 2) * 8192 + (d + 1) * 512],
                                    start=(d == 0), stop=(d == 15))
                            s.dinc(te, s.s_pe)

                @blk.scalar
                def _(sc):
                    for ig in range(16):
                        for i in range(4):
                            sc.wait_ge(s.s_pe.sem, pe0 + 4 * ig + i + 1)
                            sc.activation(
                                out=s.A_S[:, (i % 2) * 512:(i % 2 + 1) * 512],
                                in_=s.pb[i][:], func=AF.Gelu,
                                scale=s.wscal[:, 2:3])
                            sc.drain()
                            sc.activation(
                                out=s.A_BIG[:, i * 8192 + ig * 512:
                                            i * 8192 + (ig + 1) * 512],
                                in_=s.A_S[:, (i % 2) * 512:(i % 2 + 1) * 512],
                                func=AF.Gelu)
                            s.dinc(sc, s.s_ac)

            with nc.Block() as blk:
                pe0, ac0, dv0 = s.s_pe.val, s.s_ac.val, s.s_dv.val
                ss_snap = [c.val for c in s.s_ss]
                AC_I, DV_I = 6, 22

                @blk.scalar
                def _(sc):
                    for j in range(2):
                        sc.wait_ge(s.s_ss[j].sem, ss_snap[j])
                    for i in range(4):
                        if i >= 1:
                            sc.wait_ge(s.s_dv.sem, dv0 + DV_I * (i - 1) + 6)
                        for c in range(4):
                            sc.activation(
                                out=s.A_S[:, c * 2048:(c + 1) * 2048],
                                in_=s.A_BIG[:, i * 8192 + c * 2048:
                                            i * 8192 + (c + 1) * 2048],
                                func=AF.Square,
                                accum_out=s.stc[:, 40 + c:41 + c])
                        s.dinc(sc, s.s_ac)
                        sc.wait_ge(s.s_dv.sem, dv0 + DV_I * i + 1)
                        sc.activation(out=s.stc[:, 46:47], in_=s.stc[:, 44:45],
                                      func=AF.Sqrt)
                        s.dinc(sc, s.s_ac)
                        for c in range(4):
                            sc.wait_ge(s.s_dv.sem, dv0 + DV_I * i + 2
                                       + (c if c > 0 else 0))
                            sc.activation(
                                out=s.A_S[:, 2048:4096],
                                in_=s.A_BIG[:, i * 8192 + c * 2048:
                                            i * 8192 + (c + 1) * 2048],
                                func=AF.Identity, scale=s.stc[:, 49:50])
                            s.dinc(sc, s.s_ac)

                @blk.vector
                def _(v):
                    for i in range(4):
                        v.wait_ge(s.s_ac.sem, ac0 + AC_I * i + 1)
                        v.tensor_add(s.stc[:, 50:51], s.stc[:, 40:41],
                                     s.stc[:, 41:42])
                        v.tensor_add(s.stc[:, 51:52], s.stc[:, 42:43],
                                     s.stc[:, 43:44])
                        v.drain()
                        v.tensor_add(s.stc[:, 44:45], s.stc[:, 50:51],
                                     s.stc[:, 51:52])
                        v.reduce_max(out=s.stc[:, 45:46],
                                     in_=s.A_BIG[:, i * 8192:(i + 1) * 8192],
                                     axis=AX.X, apply_absolute_value=True)
                        s.dinc(v, s.s_dv)
                        v.wait_ge(s.s_ac.sem, ac0 + AC_I * i + 2)
                        v.tensor_scalar_max(s.stc[:, 47:48], s.stc[:, 46:47],
                                            1e-12)
                        v.drain()
                        v.reciprocal(out=s.stc[:, 48:49], in_=s.stc[:, 47:48])
                        v.drain()
                        v.tensor_scalar_mul(s.stc[:, 47:48], s.stc[:, 48:49],
                                            SDI)
                        v.drain()
                        v.tensor_mul(s.stc[:, 52:53], s.stc[:, 45:46],
                                     s.stc[:, 47:48])
                        v.drain()
                        v.tensor_scalar_max(s.stc[:, 53:54], s.stc[:, 52:53],
                                            1e-5)
                        v.drain()
                        v.reciprocal(out=s.stc[:, 54:55], in_=s.stc[:, 53:54])
                        v.drain()
                        v.tensor_scalar_mul(s.stc[:, 55:56], s.stc[:, 54:55],
                                            127.0)
                        v.drain()
                        v.tensor_mul(s.stc[:, 49:50], s.stc[:, 47:48],
                                     s.stc[:, 55:56])
                        v.tensor_mul(s.stc[:, 56:57], s.stc[:, 53:54],
                                     s.wscal[:, 3:4])
                        v.drain()
                        v.tensor_scalar_mul(s.stc[:, 32 + tts[i]:
                                                  33 + tts[i]],
                                            s.stc[:, 56:57], 1.0 / 127.0)
                        s.dinc(v, s.s_dv)
                        for c in range(4):
                            v.wait_ge(s.s_ac.sem, ac0 + AC_I * i + 3 + c)
                            v.tensor_copy(out=s.A_I[:], in_=s.A_S[:, 2048:4096])
                            v.drain()
                            v.tensor_copy(
                                out=s.A_BIG[:, i * 8192 + c * 2048:
                                            i * 8192 + (c + 1) * 2048],
                                in_=s.A_I[:])
                            s.dinc(v, s.s_dv)
                        for cb in range(16):
                            b = i * 16 + cb
                            v.wait_ge(s.s_pe.sem, pe0 + b + 1)
                            dst = bass.AP(s.A_BIG, i * 8192 + cb * 512,
                                          [[32768, 128], [128, 4], [1, 128]])
                            src = bass.AP(s.pb[4 + b % 2], 0,
                                          [[512, 128], [128, 4], [1, 128]])
                            v.tensor_copy(out=dst, in_=src)
                            s.dinc(v, s.s_dv)

                @blk.tensor
                def _(te):
                    for i in range(4):
                        for cb in range(16):
                            b = i * 16 + cb
                            te.wait_ge(s.s_dv.sem,
                                       dv0 + DV_I * i + 3 + (cb // 4))
                            if b >= 2:
                                pb_, pc = divmod(b - 2, 16)
                                te.wait_ge(s.s_dv.sem,
                                           dv0 + DV_I * pb_ + 7 + pc)
                            for j in range(4):
                                ch = cb * 4 + j
                                te.matmul(
                                    s.pb[4 + b % 2][:, j * 128:(j + 1) * 128],
                                    s.A_BIG[:, i * 8192 + ch * 128:
                                            i * 8192 + (ch + 1) * 128],
                                    s.ident[:], start=True, stop=True)
                            s.dinc(te, s.s_pe)

            with nc.Block() as blk:
                pe0, ac0 = s.s_pe.val, s.s_ac.val
                sl0 = [c.val for c in s.s_sl]

                @blk.sync
                def _(sy):
                    for dg in range(4):
                        for ch in range(64):
                            k = dg * 64 + ch
                            if k >= 2:
                                sy.wait_ge(s.s_pe.sem, pe0 + k - 1)
                            s.s_sl[2 + k % 2].inc(sy.dma_start(
                                out=s.A_W[:, (k % 2) * 512:
                                          (k % 2 + 1) * 512],
                                in_=t["w2q_g"].ap()[ch * 128:(ch + 1) * 128,
                                                    dg * 512:(dg + 1) * 512]),
                                16)

                @blk.tensor
                def _(te):
                    for dg in range(4):
                        for ch in range(64):
                            k = dg * 64 + ch
                            te.wait_ge(s.s_sl[2 + k % 2].sem,
                                       sl0[2 + k % 2] + 16 * (k // 2 + 1))
                            if ch == 0 and dg > 0:
                                te.wait_ge(s.s_ac.sem, ac0 + 4 * dg)
                            for i in range(4):
                                te.matmul(
                                    s.pb[i][:],
                                    s.A_BIG[:, i * 8192 + ch * 128:
                                            i * 8192 + (ch + 1) * 128],
                                    s.A_W[:, (k % 2) * 512:(k % 2 + 1) * 512],
                                    start=(ch == 0), stop=(ch == 63))
                            s.dinc(te, s.s_pe)

                @blk.scalar
                def _(sc):
                    for dg in range(4):
                        sc.wait_ge(s.s_pe.sem, pe0 + 64 * (dg + 1))
                        for i in range(4):
                            sc.activation(
                                out=s.A_B[:, i * 2048 + dg * 512:
                                          i * 2048 + (dg + 1) * 512],
                                in_=s.pb[i][:], func=AF.Identity,
                                scale=s.stc[:, 32 + tts[i]:33 + tts[i]])
                            s.dinc(sc, s.s_ac)

            with nc.Block() as blk:
                ac0, dv0 = s.s_ac.val, s.s_dv.val
                sl0 = [c.val for c in s.s_sl]
                ss0 = [c.val for c in s.s_ss]

                @blk.sync
                def _(sy, quad=quad):
                    for i, tt in enumerate(tts):
                        if i >= 2:
                            sy.wait_ge(s.s_dv.sem, dv0 + 3 * (i - 2) + 3)
                        s.s_sl[i % 2].inc(sy.dma_start(
                            out=s.A_S[:, 2048:4096],
                            in_=t["x2_dram"].ap()[tt * 128:(tt + 1) * 128, :]),
                            16)
                        sy.wait_ge(s.s_dv.sem, dv0 + 3 * i + 3)
                        s.s_ss[i % 2].inc(sy.dma_start(
                            out=t["out"].ap()[tt * 128:(tt + 1) * 128, :],
                            in_=s.A_S[:, 4096:6144]), 16)

                @blk.vector
                def _(v):
                    def ln_stats(src, varcol):
                        v.reduce_sum(out=s.stc[:, 0:1], in_=src, axis=AX.X)
                        v.tensor_mul(s.A_S[:, 6144:8192], src, src)
                        v.drain()
                        v.reduce_sum(out=s.stc[:, 1:2],
                                     in_=s.A_S[:, 6144:8192], axis=AX.X)
                        v.tensor_scalar_mul(s.stc[:, 2:3], s.stc[:, 0:1],
                                            1.0 / D)
                        v.drain()
                        v.tensor_scalar_mul(s.stc[:, 5:6], s.stc[:, 1:2],
                                            1.0 / D)
                        v.tensor_mul(s.stc[:, 6:7], s.stc[:, 2:3],
                                     s.stc[:, 2:3])
                        v.drain()
                        v.tensor_sub(s.stc[:, varcol:varcol + 1],
                                     s.stc[:, 5:6], s.stc[:, 6:7])

                    for i in range(4):
                        fsrc = s.A_B[:, i * 2048:(i + 1) * 2048]
                        ln_stats(fsrc, 7)
                        s.dinc(v, s.s_dv)
                        v.wait_ge(s.s_ac.sem, ac0 + 2 * i + 1)
                        v.reciprocal(out=s.stc[:, 3:4], in_=s.stc[:, 9:10])
                        v.drain()
                        v.tensor_mul(s.stc[:, 8:9], s.stc[:, 2:3],
                                     s.stc[:, 3:4])
                        v.drain()
                        v.tensor_scalar_mul(s.stc[:, 4:5], s.stc[:, 8:9],
                                            -1.0)
                        s.dinc(v, s.s_dv)
                        v.wait_ge(s.s_ac.sem, ac0 + 2 * i + 2)
                        v.wait_ge(s.s_sl[i % 2].sem,
                                  sl0[i % 2] + 16 * (i // 2 + 1))
                        if i >= 1:
                            v.wait_ge(s.s_ss[(i - 1) % 2].sem,
                                      ss0[(i - 1) % 2]
                                      + 16 * ((i - 1) // 2 + 1))
                        v.tensor_add(s.A_S[:, 4096:6144], s.A_S[:, 0:2048],
                                     s.A_S[:, 2048:4096])
                        s.dinc(v, s.s_dv)

                @blk.scalar
                def _(sc):
                    for i in range(4):
                        sc.wait_ge(s.s_dv.sem, dv0 + 3 * i + 1)
                        sc.activation(out=s.stc[:, 9:10], in_=s.stc[:, 7:8],
                                      func=AF.Sqrt, bias=s.eps[:, 0:1])
                        s.dinc(sc, s.s_ac)
                        sc.wait_ge(s.s_dv.sem, dv0 + 3 * i + 2)
                        sc.activation(out=s.A_S[:, 0:2048],
                                      in_=s.A_B[:, i * 2048:(i + 1) * 2048],
                                      func=AF.Identity, scale=s.stc[:, 3:4],
                                      bias=s.stc[:, 4:5])
                        s.dinc(sc, s.s_ac)

    def epilogue(self):
        nc, s = self.nc, self
        with nc.Block() as blk:
            @blk.gpsimd
            def _(g):
                g.wait_ge(s.s_st.sem, s.s_st.val)
                g.wait_ge(s.s_cc.sem, s.s_cc.val)
                for i in range(4):
                    g.wait_ge(s.s_ss[i].sem, s.s_ss[i].val)
                    g.wait_ge(s.s_sl[i].sem, s.s_sl[i].val)
                g.wait_ge(s.s_ld.sem, s.s_ld.val)


def stage_inputs(inputs):
    x = np.ascontiguousarray(inputs["L0_attn_query"])
    w_q = inputs["L0_attn_w_q"]; w_k = inputs["L0_attn_w_k"]
    w_v = inputs["L0_attn_w_v"]; w_o = inputs["L0_attn_w_o"]
    b_q = inputs["L0_attn_b_q"]; b_k = inputs["L0_attn_b_k"]
    b_v = inputs["L0_attn_b_v"]; b_o = inputs["L0_attn_b_o"]
    w1t = np.ascontiguousarray(np.asarray(inputs["L0_ffn_w1"]).T)
    w2t = np.ascontiguousarray(np.asarray(inputs["L0_ffn_w2"]).T)

    bias_pp = np.empty((128, 48), np.float32)
    for j in range(16):
        bias_pp[:, j] = b_q[j * 128:(j + 1) * 128]
        bias_pp[:, 16 + j] = b_k[j * 128:(j + 1) * 128]
        bias_pp[:, 32 + j] = b_v[j * 128:(j + 1) * 128]
    bo_row = np.asarray(b_o, np.float32).reshape(1, D)
    ident = np.eye(128, dtype=ml_dtypes.bfloat16)
    tri = np.tril(np.ones((128, 128), np.float32)).T
    tri = np.ascontiguousarray(np.tril(np.ones((128, 128), np.float32)).T
                               ).astype(ml_dtypes.bfloat16)

    maps = []
    for c in range(8):
        b, cg = c // 4, c % 4
        hoffs = np.zeros((1, 80), np.int32)
        for tl in range(4):
            hg = 4 * cg + tl
            for p in range(3):
                row = hg * 384 + p * 128
                hoffs[0, tl * 3 + p] = row * TOK
        hoffs[0, 48] = cg * TOK
        m = dict(
            x=np.ascontiguousarray(x[b, cg * TOK:(cg + 1) * TOK, :]),
            wq_sh=np.ascontiguousarray(w_q[c * 256:(c + 1) * 256, :]),
            wk_sh=np.ascontiguousarray(w_k[c * 256:(c + 1) * 256, :]),
            wv_sh=np.ascontiguousarray(w_v[c * 256:(c + 1) * 256, :]),
            wo_sh=np.ascontiguousarray(w_o[c * 256:(c + 1) * 256, :]),
            w1t_sh=np.ascontiguousarray(w1t[c * 256:(c + 1) * 256, :]),
            w2t_sh=np.ascontiguousarray(w2t[c * 1024:(c + 1) * 1024, :]),
            bias_pp=bias_pp, bo_row=bo_row, ident=ident, tri=tri,
            hoffs=hoffs,
        )
        maps.append({k: np.ascontiguousarray(v) for k, v in m.items()})
    return maps


def kernel(**inputs):
    maps = stage_inputs(inputs)
    bld = Builder()
    nc = bld.build()
    res = run_bass_kernel_spmd(nc, maps, core_ids=list(range(8)))
    out = np.empty((B, T, D), np.float32)
    for c in range(8):
        b, cg = c // 4, c % 4
        out[b, cg * TOK:(cg + 1) * TOK, :] = res.results[c]["out"]
    if _DEBUG:
        return out, [
            {k: r[k] for k in r if k.startswith("dbg_")} for r in res.results]
    return out


# version 02 (final)
# speedup vs baseline: 1.0341x; 1.0341x over previous
import numpy as np
import ml_dtypes
from contextlib import ExitStack

import concourse.bass as bass
import concourse.mybir as mybir
from concourse.bass_utils import run_bass_kernel_spmd

F32 = mybir.dt.float32
BF16 = mybir.dt.bfloat16
I32 = mybir.dt.int32
AF = mybir.ActivationFunctionType
ALU = mybir.AluOpType
AX = mybir.AxisListType

B, T, D = 2, 4096, 2048
H, HD = 16, 128
INNER = 8192
TOK = 1024
NT = TOK // 128
SCL = float(1.0 / np.sqrt(128.0))
N_W1 = float(D * INNER)
GROUP4 = [[0, 1, 2, 3], [4, 5, 6, 7]]
GROUP8 = [[0, 1, 2, 3, 4, 5, 6, 7]]

_DEBUG = ()
_BUILD_TO = 99
_RACECHECK = False
_S2_HEADS = 4
_S2_QGS = 8


class Ctr:
    def __init__(self, sem):
        self.sem = sem
        self.val = 0

    def inc(self, instr, n=1):
        instr.then_inc(self.sem, n)
        self.val += n
        return self.val


class Builder:
    def __init__(self):
        self.nc = bass.Bass("TRN2", target_bir_lowering=False,
                            detect_race_conditions=_RACECHECK)
        self.dbg = []
        nc = self.nc
        t = {}
        t["x"] = nc.dram_tensor("x", [TOK, D], F32, kind="ExternalInput")
        for n, shp in [("wq_sh", [256, D]), ("wk_sh", [256, D]),
                       ("wv_sh", [256, D]), ("wo_sh", [256, D]),
                       ("w1t_sh", [256, INNER]), ("w2t_sh", [1024, D]),
                       ("bias_pp", [128, 48]), ("bo_row", [1, D])]:
            t[n] = nc.dram_tensor(n, shp, F32, kind="ExternalInput")
        t["ident"] = nc.dram_tensor("ident", [128, 128], BF16, kind="ExternalInput")
        t["tri"] = nc.dram_tensor("tri", [128, 128], BF16, kind="ExternalInput")
        t["hoffs"] = nc.dram_tensor("hoffs", [1, 80], I32, kind="ExternalInput")
        t["out"] = nc.dram_tensor("out", [TOK, D], F32, kind="ExternalOutput")

        t["qkv_in"] = nc.dram_tensor("qkv_in", [6144, TOK], BF16)
        t["qkv_g"] = nc.dram_tensor("qkv_g", [4 * 6144, TOK], BF16)
        t["at_in"] = nc.dram_tensor("at_in", [512, T], BF16)
        t["at_g"] = nc.dram_tensor("at_g", [D, T], BF16)
        t["wat_in"] = nc.dram_tensor("wat_in", [1024, D], BF16)
        t["wat_g"] = nc.dram_tensor("wat_g", [8192, D], BF16)
        t["w1q_in"] = nc.dram_tensor("w1q_in", [256, INNER], BF16)
        t["w1q_g"] = nc.dram_tensor("w1q_g", [D, INNER], BF16)
        t["w2q_in"] = nc.dram_tensor("w2q_in", [1024, D], BF16)
        t["w2q_g"] = nc.dram_tensor("w2q_g", [INNER, D], BF16)
        t["wsum_in"] = nc.dram_tensor("wsum_in", [1, 2], F32)
        t["fence_in"] = nc.dram_tensor("fence_in", [1, 2], F32)
        t["fence_out"] = nc.dram_tensor("fence_out", [1, 2], F32)
        t["wsum_g"] = nc.dram_tensor("wsum_g", [1, 2], F32)
        t["x2_dram"] = nc.dram_tensor("x2_dram", [TOK, D], F32)
        t["hqt_dram"] = nc.dram_tensor("hqt_dram", [INNER // 128, 128, TOK], BF16)
        self.t = t

    def cc_wait_all(self, g):
        g.wait_ge(self.s_cc.sem, self.s_cc.val)

    def cc_fence(self, g):
        t, s = self.t, self
        self.cc_wait_all(g)
        s.s_cc.inc(g.collective_compute(
            "AllReduce", ALU.add, replica_groups=GROUP8,
            ins=[t["fence_in"].ap().opt()], outs=[t["fence_out"].ap().opt()]))

    def dinc(self, eng, ctr, n=1):
        ctr.inc(eng.drain(fusable=False), n)

    def dump(self, name, shape, dt, src_ap_fn):
        if name not in _DEBUG:
            return
        d = self.nc.dram_tensor("dbg_" + name, shape, dt, kind="ExternalOutput")
        s = self
        with self.nc.Block() as blk:
            @blk.sync
            def _(sy):
                sy.wait_ge(s.s_cc.sem, s.s_cc.val)
                sy.wait_ge(s.s_st.sem, s.s_st.val)
                sy.wait_ge(s.s_ld.sem, s.s_ld.val)
                for i in range(4):
                    sy.wait_ge(s.s_ss[i].sem, s.s_ss[i].val)
                s.s_st.inc(sy.dma_start(out=d.ap(), in_=src_ap_fn()), 16)
                sy.wait_ge(s.s_st.sem, s.s_st.val)
        self.dbg.append("dbg_" + name)

    def build(self):
        nc, t = self.nc, self.t
        with ExitStack() as es:
            def sbt(name, shape, dt=F32):
                return es.enter_context(nc.sbuf_tensor(name, shape, dt))
            s = self
            s.A_BIG = sbt("A_BIG", [128, 32768], BF16)
            s.A_A = sbt("A_A", [128, 16384], BF16)
            s.A_S = sbt("A_S", [128, 8192], F32)
            s.A_B = sbt("A_B", [128, 8192], BF16)
            s.A_W = sbt("A_W", [128, 16384], BF16)
            s.A_I = sbt("A_I", [128, 2048], I32)
            s.ident = sbt("ident_sb", [128, 128], BF16)
            s.tri = sbt("tri_sb", [128, 128], BF16)
            s.ones_b = sbt("ones_b", [128, 128], BF16)
            s.ones_f = sbt("ones_f", [128, 128], F32)
            s.biasq = sbt("biasq", [128, 48], F32)
            s.bo_sb = sbt("bo_sb", [1, D], F32)
            s.bo_bf = sbt("bo_bf", [1, D], BF16)
            s.offs = sbt("offs_sb", [1, 80], I32)
            s.stc = sbt("stc", [128, 64], F32)
            s.st1 = sbt("st1", [1, 512], F32)
            s.wscal = sbt("wscal", [128, 8], F32)
            s.rown = sbt("rown", [1, 64], F32)
            s.rowl = sbt("rowl", [1, 512], F32)
            s.eps = sbt("eps_sb", [128, 2], F32)
            s.negc = sbt("negc", [128, 1], F32)
            s.pb = [es.enter_context(nc.psum_tensor(f"pb{i}", [128, 512], F32))
                    for i in range(8)]
            for nm in ["s_ld", "s_pe", "s_ac", "s_dv", "s_pl", "s_cc",
                       "s_st", "s_w"]:
                setattr(s, nm, Ctr(es.enter_context(nc.semaphore(nm))))
            s.s_sl = [Ctr(es.enter_context(nc.semaphore(f"s_sl{i}")))
                      for i in range(4)]
            s.s_ss = [Ctr(es.enter_context(nc.semaphore(f"s_ss{i}")))
                      for i in range(4)]
            s.roff = [nc.alloc_register(mybir.EngineType.SP, f"roff{i}")
                      for i in range(4)]

            self.p0a_consts_wat()
            self.p0b_wsums()
            self.p0c_quant()
            if _BUILD_TO >= 1:
                self.p1_ln1()
                self.p1_qkv()
            if _BUILD_TO >= 2:
                self.s2_attention()
            if _BUILD_TO >= 3:
                self.s3a_oproj()
                self.s3b_ln23()
                self.s3c_ffn()
            self.epilogue()
        return nc

    def p0a_consts_wat(self):
        nc, t, s = self.nc, self.t, self
        shards = [t["wq_sh"], t["wk_sh"], t["wv_sh"], t["wo_sh"]]
        for half in range(2):
            with nc.Block() as blk:
                @blk.sync
                def _(sy, half=half):
                    if half == 0:
                        for src, dst in [(t["ident"], s.ident), (t["tri"], s.tri),
                                         (t["bias_pp"], s.biasq),
                                         (t["bo_row"], s.bo_sb),
                                         (t["hoffs"], s.offs)]:
                            s.s_ld.inc(sy.dma_start(out=dst[:], in_=src.ap()), 16)
                    for j in range(4):
                        k = half * 4 + j
                        wsh, rt = shards[k // 2], k % 2
                        s.s_ld.inc(sy.dma_start(
                            out=s.A_S[:, j * 2048:(j + 1) * 2048],
                            in_=wsh.ap()[rt * 128:(rt + 1) * 128, :]), 16)

            with nc.Block() as blk:
                dv0 = s.s_dv.val
                ld_after = s.s_ld.val

                st_before = s.s_st.val

                @blk.vector
                def _(v, half=half, ld_after=ld_after, st_before=st_before):
                    v.wait_ge(s.s_ld.sem, ld_after)
                    v.wait_ge(s.s_st.sem, st_before)
                    if half == 0:
                        v.memset(s.ones_b[:], 1.0)
                        v.memset(s.ones_f[:], 1.0)
                        v.memset(s.eps[:, 0:1], 1e-5)
                        v.memset(s.eps[:, 1:2], 1e-12)
                        v.tensor_copy(out=s.bo_bf[:], in_=s.bo_sb[:])
                    for j in range(4):
                        v.tensor_copy(
                            out=s.A_B[:, j * 2048:(j + 1) * 2048],
                            in_=s.A_S[:, j * 2048:(j + 1) * 2048])
                        s.dinc(v, s.s_dv)

                @blk.sync
                def _(sy, half=half, dv0=dv0):
                    for j in range(4):
                        k = half * 4 + j
                        sy.wait_ge(s.s_dv.sem, dv0 + j + 1)
                        s.s_st.inc(sy.dma_start(
                            out=t["wat_in"].ap()[k * 128:(k + 1) * 128, :],
                            in_=s.A_B[:, j * 2048:(j + 1) * 2048]), 16)

        if "marker" in _DEBUG:
            with nc.Block() as blk:
                @blk.vector
                def _(v):
                    v.memset(s.A_W[:, 0:1024], 3.0)
                    s.dinc(v, s.s_dv)

                @blk.sync
                def _(sy):
                    sy.wait_ge(s.s_dv.sem, s.s_dv.val)
                    for r in range(48):
                        s.s_st.inc(sy.dma_start(
                            out=t["qkv_in"].ap()[r * 128:(r + 1) * 128, :],
                            in_=s.A_W[:, 0:1024]), 16)
                    sy.wait_ge(s.s_st.sem, s.s_st.val)
        with nc.Block() as blk:
            @blk.sync
            def _(sy):
                s.s_st.inc(sy.dma_start(out=t["fence_in"].ap(),
                                        in_=s.eps[0:1, 0:2]), 16)

            @blk.gpsimd
            def _(g):
                g.wait_ge(s.s_st.sem, s.s_st.val)
                s.s_cc.inc(g.collective_compute(
                    "AllGather", ALU.bypass, replica_groups=GROUP8,
                    ins=[t["wat_in"].ap().opt()],
                    outs=[t["wat_g"].ap().opt()]))
        self.cc_wat = s.s_cc.val

    def p0b_wsums(self):
        nc, t, s = self.nc, self.t, self
        chunks = ([("w1t_sh", rt, cc) for rt in range(2) for cc in range(4)] +
                  [("w2t_sh", rt, 0) for rt in range(8)])
        nW1 = 8
        with nc.Block() as blk:
            ac0, pe0 = s.s_ac.val, s.s_pe.val
            sl0 = [c.val for c in s.s_sl]

            @blk.sync
            def _(sy):
                for k, (nm, rt, cc) in enumerate(chunks):
                    if k >= 2:
                        sy.wait_ge(s.s_ac.sem, ac0 + k - 1)
                    s.s_sl[k % 2].inc(sy.dma_start(
                        out=s.A_S[:, (k % 2) * 2048:(k % 2 + 1) * 2048],
                        in_=t[nm].ap()[rt * 128:(rt + 1) * 128,
                                       cc * 2048:(cc + 1) * 2048]), 16)

            @blk.scalar
            def _(sc):
                for k in range(len(chunks)):
                    sc.wait_ge(s.s_sl[k % 2].sem, sl0[k % 2] + 16 * (k // 2 + 1))
                    if k >= 2:
                        sc.wait_ge(s.s_pe.sem, pe0 + k - 1)
                    sc.activation(
                        out=s.A_S[:, 4096 + (k % 2) * 2048:
                                  4096 + (k % 2 + 1) * 2048],
                        in_=s.A_S[:, (k % 2) * 2048:(k % 2 + 1) * 2048],
                        func=AF.Abs)
                    s.dinc(sc, s.s_ac)

            @blk.tensor
            def _(te):
                for k in range(len(chunks)):
                    te.wait_ge(s.s_ac.sem, ac0 + k + 1)
                    bank = s.pb[0] if k < nW1 else s.pb[1]
                    first, last = k in (0, nW1), k in (nW1 - 1, len(chunks) - 1)
                    for q in range(4):
                        ins = te.matmul(
                            bank[0:1, :], s.ones_f[:, 0:1],
                            s.A_S[:, 4096 + (k % 2) * 2048 + q * 512:
                                  4096 + (k % 2) * 2048 + (q + 1) * 512],
                            start=(first and q == 0), stop=(last and q == 3))
                    s.dinc(te, s.s_pe)

        with nc.Block() as blk:
            @blk.vector
            def _(v):
                v.reduce_sum(out=s.st1[0:1, 0:1], in_=s.pb[0][0:1, :], axis=AX.X)
                v.reduce_sum(out=s.st1[0:1, 1:2], in_=s.pb[1][0:1, :], axis=AX.X)
                s.dinc(v, s.s_dv)

            @blk.sync
            def _(sy):
                sy.wait_ge(s.s_dv.sem, s.s_dv.val)
                s.s_st.inc(sy.dma_start(out=t["wsum_in"].ap(),
                                        in_=s.st1[0:1, 0:2]), 16)

            @blk.gpsimd
            def _(g):
                g.wait_ge(s.s_st.sem, s.s_st.val)
                s.s_cc.inc(g.collective_compute(
                    "AllReduce", ALU.add, replica_groups=GROUP8,
                    ins=[t["wsum_in"].ap().opt()],
                    outs=[t["wsum_g"].ap().opt()]))

            @blk.sync
            def _(sy):
                sy.wait_ge(s.s_cc.sem, s.s_cc.val)
                s.s_ld.inc(sy.dma_start(out=s.st1[0:1, 8:10],
                                        in_=t["wsum_g"].ap()), 16)

        with nc.Block() as blk:
            ld_now = s.s_ld.val

            @blk.vector
            def _(v, ld_now=ld_now):
                v.wait_ge(s.s_ld.sem, ld_now)
                v.tensor_scalar(out=s.st1[0:1, 12:13], in0=s.st1[0:1, 8:9],
                                scalar1=1.0 / N_W1, scalar2=1e-5,
                                op0=ALU.mult, op1=ALU.max)
                v.tensor_scalar(out=s.st1[0:1, 13:14], in0=s.st1[0:1, 9:10],
                                scalar1=1.0 / N_W1, scalar2=1e-5,
                                op0=ALU.mult, op1=ALU.max)
                v.drain()
                v.reciprocal(out=s.st1[0:1, 10:11], in_=s.st1[0:1, 12:13])
                v.reciprocal(out=s.st1[0:1, 11:12], in_=s.st1[0:1, 13:14])
                s.dinc(v, s.s_dv)

            @blk.tensor
            def _(te):
                te.wait_ge(s.s_dv.sem, s.s_dv.val)
                te.matmul(s.pb[2][:, 0:4], s.ones_f[0:1, :],
                          s.st1[0:1, 10:14], start=True, stop=True)
                s.dinc(te, s.s_pe)

            @blk.scalar
            def _(sc):
                sc.wait_ge(s.s_pe.sem, s.s_pe.val)
                s.s_ac.inc(sc.activation(out=s.wscal[:, 0:4],
                                         in_=s.pb[2][:, 0:4], func=AF.Copy))

    def p0c_quant(self):
        nc, t, s = self.nc, self.t, self
        chunks = ([("w1t_sh", "w1q_in", rt, cc, 0) for rt in range(2)
                   for cc in range(4)] +
                  [("w2t_sh", "w2q_in", rt, 0, 1) for rt in range(8)])
        with nc.Block() as blk:
            dv0, st0 = s.s_dv.val, s.s_st.val
            sl0 = [c.val for c in s.s_sl]
            ss0 = [c.val for c in s.s_ss]

            @blk.sync
            def _(sy):
                for k, (nm, _, rt, cc, _j) in enumerate(chunks):
                    if k >= 2:
                        sy.wait_ge(s.s_dv.sem, dv0 + 3 * (k - 2) + 1)
                    s.s_sl[k % 2].inc(sy.dma_start(
                        out=s.A_S[:, (k % 2) * 2048:(k % 2 + 1) * 2048],
                        in_=t[nm].ap()[rt * 128:(rt + 1) * 128,
                                       cc * 2048:(cc + 1) * 2048]), 16)

            @blk.vector
            def _(v):
                for k, (_, _, _, _, j) in enumerate(chunks):
                    v.wait_ge(s.s_sl[k % 2].sem, sl0[k % 2] + 16 * (k // 2 + 1))
                    v.wait_ge(s.s_st.sem, st0)
                    if k >= 2:
                        v.wait_ge(s.s_ss[k % 2].sem,
                                  ss0[k % 2] + 16 * ((k - 2) // 2 + 1))
                    src = s.A_S[:, (k % 2) * 2048:(k % 2 + 1) * 2048]
                    v.tensor_scalar(
                        out=s.A_S[:, 4096:6144], in0=src,
                        scalar1=s.wscal[:, j:j + 1], scalar2=None,
                        op0=ALU.mult)
                    s.dinc(v, s.s_dv)
                    v.tensor_copy(out=s.A_I[:], in_=s.A_S[:, 4096:6144])
                    v.drain()
                    v.tensor_scalar(
                        out=s.A_B[:, (k % 2) * 2048:(k % 2 + 1) * 2048],
                        in0=s.A_I[:], scalar1=1.0, scalar2=-1.0,
                        op0=ALU.min, op1=ALU.max)
                    s.dinc(v, s.s_dv, 2)

            @blk.scalar
            def _(sc):
                for k, (_, dst, rt, cc, _j) in enumerate(chunks):
                    sc.wait_ge(s.s_dv.sem, dv0 + 3 * (k + 1))
                    s.s_ss[k % 2].inc(sc.dma_start(
                        out=t[dst].ap()[rt * 128:(rt + 1) * 128,
                                        cc * 2048:(cc + 1) * 2048],
                        in_=s.A_B[:, (k % 2) * 2048:(k % 2 + 1) * 2048]), 16)

            @blk.gpsimd
            def _(g):
                for i in range(2):
                    g.wait_ge(s.s_ss[i].sem, s.s_ss[i].val)
                s.s_cc.inc(g.collective_compute(
                    "AllGather", ALU.bypass, replica_groups=GROUP8,
                    ins=[t["w1q_in"].ap().opt()],
                    outs=[t["w1q_g"].ap().opt()]))
                s.s_cc.inc(g.collective_compute(
                    "AllGather", ALU.bypass, replica_groups=GROUP8,
                    ins=[t["w2q_in"].ap().opt()],
                    outs=[t["w2q_g"].ap().opt()]))
        self.cc_wq = s.s_cc.val
        self.dump("wq1", [D, INNER], BF16, lambda: t["w1q_g"].ap())
        self.dump("wsum", [1, 16], F32, lambda: self.st1[0:1, 0:16])

    def p1_ln1(self):
        nc, t, s = self.nc, self.t, self
        with nc.Block() as blk:
            dv0, ac0, pe0 = s.s_dv.val, s.s_ac.val, s.s_pe.val
            sl0 = [c.val for c in s.s_sl]
            DV_T = 6
            AC_T = 2
            PE_T = 4

            @blk.sync
            def _(sy):
                for tt in range(NT):
                    if tt >= 2:
                        sy.wait_ge(s.s_ac.sem, ac0 + AC_T * (tt - 1))
                    s.s_sl[tt % 2].inc(sy.dma_start(
                        out=s.A_S[:, (tt % 2) * 2048:(tt % 2 + 1) * 2048],
                        in_=t["x"].ap()[tt * 128:(tt + 1) * 128, :]), 16)

            @blk.vector
            def _(v):
                for tt in range(NT):
                    xt = s.A_S[:, (tt % 2) * 2048:(tt % 2 + 1) * 2048]
                    v.wait_ge(s.s_sl[tt % 2].sem, sl0[tt % 2] + 16 * (tt // 2 + 1))
                    v.reduce_sum(out=s.stc[:, 0:1], in_=xt, axis=AX.X)
                    v.tensor_mul(s.A_S[:, 4096:6144], xt, xt)
                    v.drain()
                    v.reduce_sum(out=s.stc[:, 1:2], in_=s.A_S[:, 4096:6144],
                                 axis=AX.X)
                    v.tensor_scalar_mul(s.stc[:, 2:3], s.stc[:, 0:1], 1.0 / D)
                    v.drain()
                    v.tensor_scalar_mul(s.stc[:, 5:6], s.stc[:, 1:2], 1.0 / D)
                    v.tensor_mul(s.stc[:, 6:7], s.stc[:, 2:3], s.stc[:, 2:3])
                    v.drain()
                    v.tensor_sub(s.stc[:, 7:8], s.stc[:, 5:6], s.stc[:, 6:7])
                    s.dinc(v, s.s_dv)
                    v.wait_ge(s.s_ac.sem, ac0 + AC_T * tt + 1)
                    v.reciprocal(out=s.stc[:, 3:4], in_=s.stc[:, 9:10])
                    v.drain()
                    v.tensor_mul(s.stc[:, 8:9], s.stc[:, 2:3], s.stc[:, 3:4])
                    v.drain()
                    v.tensor_scalar_mul(s.stc[:, 4:5], s.stc[:, 8:9], -1.0)
                    s.dinc(v, s.s_dv)
                    for qb in range(4):
                        v.wait_ge(s.s_pe.sem, pe0 + PE_T * tt + qb + 1)
                        bank = s.pb[4 + qb % 2]
                        dst = bass.AP(s.A_A, qb * 4096 + tt * 128,
                                      [[16384, 128], [1024, 4], [1, 128]])
                        src = bass.AP(bank, 0,
                                      [[512, 128], [128, 4], [1, 128]])
                        v.tensor_copy(out=dst, in_=src)
                        s.dinc(v, s.s_dv)

            @blk.scalar
            def _(sc):
                for tt in range(NT):
                    xt = s.A_S[:, (tt % 2) * 2048:(tt % 2 + 1) * 2048]
                    sc.wait_ge(s.s_dv.sem, dv0 + DV_T * tt + 1)
                    sc.activation(out=s.stc[:, 9:10], in_=s.stc[:, 7:8],
                                  func=AF.Sqrt, bias=s.eps[:, 0:1])
                    s.dinc(sc, s.s_ac)
                    sc.wait_ge(s.s_dv.sem, dv0 + DV_T * tt + 2)
                    sc.activation(
                        out=s.A_B[:, (tt % 2) * 2048:(tt % 2 + 1) * 2048],
                        in_=xt, func=AF.Identity, scale=s.stc[:, 3:4],
                        bias=s.stc[:, 4:5])
                    s.dinc(sc, s.s_ac)

            @blk.tensor
            def _(te):
                for tt in range(NT):
                    te.wait_ge(s.s_ac.sem, ac0 + AC_T * tt + 2)
                    for qb in range(4):
                        bank = s.pb[4 + qb % 2]
                        if tt * 4 + qb >= 2:
                            pt, pq = divmod(tt * 4 + qb - 2, 4)
                            te.wait_ge(s.s_dv.sem,
                                       dv0 + DV_T * pt + 2 + pq + 1)
                        for j in range(4):
                            dch = qb * 4 + j
                            ins = te.matmul(
                                bank[:, j * 128:(j + 1) * 128],
                                s.A_B[:, (tt % 2) * 2048 + dch * 128:
                                      (tt % 2) * 2048 + (dch + 1) * 128],
                                s.ident[:], start=True, stop=True)
                        s.dinc(te, s.s_pe)
        self.ln1_dv_done = s.s_dv.val
        self.dump("xnT", [128, 16384], BF16, lambda: s.A_A[:, 0:16384])

    def p1_qkv(self):
        nc, t, s = self.nc, self.t, self
        with nc.Block() as blk:
            ld0, pe0, ac0 = s.s_ld.val, s.s_pe.val, s.s_ac.val
            ss0 = [c.val for c in s.s_ss]

            @blk.sync
            def _(sy):
                sy.wait_ge(s.s_cc.sem, self.cc_wat)
                for proj in range(3):
                    if proj > 0:
                        sy.wait_ge(s.s_pe.sem, pe0 + 32 * proj)
                    for d in range(16):
                        grow = (d // 2) * 1024 + proj * 256 + (d % 2) * 128
                        s.s_ld.inc(sy.dma_start(
                            out=s.A_BIG[:, d * 2048:(d + 1) * 2048],
                            in_=t["wat_g"].ap()[grow:grow + 128, :]), 16)
                    for oc in range(16):
                        go = proj * 16 + oc
                        sy.wait_ge(s.s_ac.sem, ac0 + 2 * go + 2)
                        row = oc * 384 + proj * 128
                        s.s_ss[go % 2].inc(sy.dma_start(
                            out=t["qkv_in"].ap()[row:row + 128, :],
                            in_=s.A_B[:, (go % 2) * 1024:
                                      (go % 2 + 1) * 1024]), 16)

            @blk.tensor
            def _(te):
                te.wait_ge(s.s_dv.sem, self.ln1_dv_done)
                g = 0
                for proj in range(3):
                    te.wait_ge(s.s_ld.sem, ld0 + 16 * 16 * (proj + 1))
                    for oc in range(16):
                        for th in range(2):
                            bank = s.pb[g % 4]
                            if g >= 4:
                                te.wait_ge(s.s_ac.sem, ac0 + g - 3)
                            for d in range(16):
                                ins = te.matmul(
                                    bank[:],
                                    s.A_BIG[:, d * 2048 + oc * 128:
                                            d * 2048 + (oc + 1) * 128],
                                    s.A_A[:, d * 1024 + th * 512:
                                          d * 1024 + (th + 1) * 512],
                                    start=(d == 0), stop=(d == 15))
                            s.dinc(te, s.s_pe)
                            g += 1

            @blk.scalar
            def _(sc):
                g = 0
                for proj in range(3):
                    for oc in range(16):
                        go = proj * 16 + oc
                        for th in range(2):
                            sc.wait_ge(s.s_pe.sem, pe0 + g + 1)
                            if th == 0 and go >= 2:
                                sc.wait_ge(s.s_ss[go % 2].sem,
                                           ss0[go % 2] + 16 * ((go - 2) // 2 + 1))
                            sc.activation(
                                out=s.A_B[:, (go % 2) * 1024 + th * 512:
                                          (go % 2) * 1024 + (th + 1) * 512],
                                in_=s.pb[g % 4][:], func=AF.Identity,
                                bias=s.biasq[:, proj * 16 + oc:
                                             proj * 16 + oc + 1])
                            s.dinc(sc, s.s_ac)
                            g += 1


            @blk.gpsimd
            def _(g_):
                for i in range(4):
                    g_.wait_ge(s.s_ss[i].sem, s.s_ss[i].val)
                s.s_cc.inc(g_.collective_compute(
                    "AllGather", ALU.bypass, replica_groups=GROUP4,
                    ins=[t["qkv_in"].ap().opt()],
                    outs=[t["qkv_g"].ap().opt()]))
        self.cc_qkv = s.s_cc.val
        self.dump("qkv", [6144, TOK], BF16, lambda: t["qkv_in"].ap())
        self.dump("qkvg", [4 * 6144, TOK], BF16, lambda: t["qkv_g"].ap())
        self.dump("qkv2", [6144, TOK], BF16, lambda: t["qkv_in"].ap())

    def s2_attention(self):
        nc, t, s = self.nc, self.t, self
        QO, KO, VO, VT, AT = 0, 4096, 8192, 24576, 28672
        for hl in range(_S2_HEADS):
            with nc.Block() as blk:
                pe0, ac0, dv0 = s.s_pe.val, s.s_ac.val, s.s_dv.val
                st0 = s.s_st.val
                qk0, v0 = s.s_sl[2].val, s.s_sl[3].val

                @blk.sync
                def _(sy, hl=hl):
                    if hl == 0:
                        sy.wait_ge(s.s_cc.sem, self.cc_qkv)
                    for p, base, sem in [(0, QO, s.s_sl[2]),
                                         (1, KO, s.s_sl[2]),
                                         (2, VO, s.s_sl[3])]:
                        idx = hl * 3 + p
                        if "statoff" in _DEBUG:
                            off = (hl * 384 + p * 128) * TOK
                        else:
                            sy.reg_load(s.roff[p], s.offs[0:1, idx:idx + 1])
                            off = s.roff[p]
                        src = bass.AP(t["qkv_g"], off,
                                      [[TOK, 128], [6144 * TOK, 4], [1, TOK]])
                        dst = bass.AP(s.A_BIG, base,
                                      [[32768, 128], [TOK, 4], [1, TOK]])
                        sem.inc(sy.dma_start(out=dst, in_=src,
                                             bounds_check=None), 16)

                @blk.scalar
                def _(sc, hl=hl):
                    sc.wait_ge(s.s_sl[2].sem, qk0 + 2 * 16)
                    sc.activation(out=s.A_S[:, 0:4096],
                                  in_=s.A_BIG[:, QO:QO + 4096], func=AF.Square)
                    s.dinc(sc, s.s_ac)
                    sc.activation(out=s.A_S[:, 4096:8192],
                                  in_=s.A_BIG[:, KO:KO + 4096], func=AF.Square)
                    s.dinc(sc, s.s_ac)
                    sc.wait_ge(s.s_pe.sem, pe0 + 11)
                    sc.activation(out=s.negc[:, 0:1], in_=s.pb[4][:, 0:1],
                                  func=AF.Copy)
                    s.dinc(sc, s.s_ac)
                    sc.drain()
                    pv_dinc = {}
                    pe_v = pe0 + 11
                    for qg in range(_S2_QGS):
                        nkv = 4 * qg + 4
                        for tt_ in range(nkv):
                            pe_v += 1
                            sc.wait_ge(s.s_pe.sem, pe_v)
                            if qg > 0:
                                sc.wait_ge(s.s_pe.sem, pv_dinc[qg - 1])
                            sc.activation(
                                out=s.A_A[:, tt_ * 512:(tt_ + 1) * 512],
                                in_=s.pb[tt_ % 2][:], func=AF.Exp,
                                scale=SCL, bias=s.negc[:, 0:1])
                            s.dinc(sc, s.s_ac)
                        pe_v += 1
                        pv_dinc[qg] = pe_v
                        pe_v += 1
                    self._s2_scalar_end = s.s_ac.val

                @blk.tensor
                def _(te, hl=hl):
                    te.wait_ge(s.s_ac.sem, ac0 + 1)
                    for ch in range(8):
                        te.matmul(s.pb[ch][0:1, :], s.ones_f[:, 0:1],
                                  s.A_S[:, ch * 512:(ch + 1) * 512],
                                  start=True, stop=True)
                    s.dinc(te, s.s_pe)
                    te.wait_ge(s.s_sl[3].sem, v0 + 1 * 16)
                    te.wait_ge(s.s_dv.sem, dv0 + 1)
                    for qb in range(8):
                        if qb >= 2:
                            te.wait_ge(s.s_dv.sem, dv0 + qb)
                        for j in range(4):
                            tt_ = qb * 4 + j
                            te.matmul(s.pb[4 + qb % 2][:, j * 128:(j + 1) * 128],
                                      s.A_BIG[:, VO + tt_ * 128:
                                              VO + (tt_ + 1) * 128],
                                      s.ident[:], start=True, stop=True)
                        s.dinc(te, s.s_pe)
                    te.wait_ge(s.s_ac.sem, ac0 + 2)
                    te.wait_ge(s.s_dv.sem, dv0 + 9)
                    for ch in range(8):
                        te.matmul(s.pb[ch][0:1, :], s.ones_f[:, 0:1],
                                  s.A_S[:, 4096 + ch * 512:
                                        4096 + (ch + 1) * 512],
                                  start=True, stop=True)
                    s.dinc(te, s.s_pe)
                    te.wait_ge(s.s_dv.sem, dv0 + 10)
                    te.matmul(s.pb[4][:, 0:1], s.ones_f[0:1, :],
                              s.st1[0:1, 30:31], start=True, stop=True)
                    s.dinc(te, s.s_pe)
                    ac_v = ac0 + 3
                    dv_v = dv0 + 10
                    evac_dinc = {}
                    for qg in range(_S2_QGS):
                        nkv = 4 * qg + 4
                        pvb = s.pb[2] if qg % 2 == 0 else s.pb[7]
                        lb = s.pb[3] if qg % 2 == 0 else s.pb[6]
                        for tt_ in range(nkv):
                            if tt_ >= 2:
                                te.wait_ge(s.s_ac.sem, ac_v + tt_ - 1)
                            elif qg > 0:
                                te.wait_ge(s.s_ac.sem, ac_v)
                            te.matmul(s.pb[tt_ % 2][:],
                                      s.A_BIG[:, KO + tt_ * 128:
                                              KO + (tt_ + 1) * 128],
                                      s.A_BIG[:, QO + qg * 512:
                                              QO + (qg + 1) * 512],
                                      start=True, stop=True)
                            s.dinc(te, s.s_pe)
                        if qg >= 2:
                            te.wait_ge(s.s_dv.sem, evac_dinc[qg - 2])
                        for tt_ in range(nkv):
                            te.wait_ge(s.s_ac.sem, ac_v + tt_ + 1)
                            if tt_ >= 4 * qg:
                                te.wait_ge(s.s_dv.sem,
                                           dv_v + (tt_ - 4 * qg) + 1)
                            te.matmul(pvb[:],
                                      s.A_BIG[:, VT + tt_ * 128:
                                              VT + (tt_ + 1) * 128],
                                      s.A_A[:, tt_ * 512:(tt_ + 1) * 512],
                                      start=(tt_ == 0), stop=(tt_ == nkv - 1))
                            te.matmul(lb[0:1, :], s.ones_b[:, 0:1],
                                      s.A_A[:, tt_ * 512:(tt_ + 1) * 512],
                                      start=(tt_ == 0), stop=(tt_ == nkv - 1))
                        s.dinc(te, s.s_pe)
                        te.wait_ge(s.s_dv.sem, dv_v + 4 + 1)
                        te.matmul(s.pb[4][:, 0:512], s.ones_f[0:1, :],
                                  s.rowl[0:1, :], start=True, stop=True)
                        s.dinc(te, s.s_pe)
                        ac_v += nkv
                        dv_v += 4 + 1 + 2
                        evac_dinc[qg] = dv_v

                @blk.vector
                def _(v, hl=hl):
                    v.wait_ge(s.s_st.sem, st0)
                    v.wait_ge(s.s_pe.sem, pe0 + 1)
                    for ch in range(8):
                        v.reduce_max(out=s.rown[0:1, ch:ch + 1],
                                     in_=s.pb[ch][0:1, :], axis=AX.X)
                    v.drain()
                    v.reduce_max(out=s.rown[0:1, 16:17],
                                 in_=s.rown[0:1, 0:8], axis=AX.X)
                    s.dinc(v, s.s_dv)
                    for qb in range(8):
                        v.wait_ge(s.s_pe.sem, pe0 + 2 + qb)
                        v.tensor_copy(
                            out=s.A_BIG[:, VT + qb * 512:VT + (qb + 1) * 512],
                            in_=s.pb[4 + qb % 2][:])
                        s.dinc(v, s.s_dv)
                    v.wait_ge(s.s_pe.sem, pe0 + 10)
                    for ch in range(8):
                        v.reduce_max(out=s.rown[0:1, 8 + ch:9 + ch],
                                     in_=s.pb[ch][0:1, :], axis=AX.X)
                    v.drain()
                    v.reduce_max(out=s.rown[0:1, 17:18],
                                 in_=s.rown[0:1, 8:16], axis=AX.X)
                    v.drain()
                    v.tensor_add(s.rown[0:1, 18:19], s.rown[0:1, 16:17],
                                 s.rown[0:1, 17:18])
                    v.drain()
                    v.tensor_scalar_mul(s.st1[0:1, 30:31],
                                        s.rown[0:1, 18:19], -0.5 * SCL)
                    s.dinc(v, s.s_dv)
                    ac_v = ac0 + 3
                    pe_v = pe0 + 11
                    dv_v = dv0 + 10
                    for qg in range(_S2_QGS):
                        nkv = 4 * qg + 4
                        pvb = s.pb[2] if qg % 2 == 0 else s.pb[7]
                        lb = s.pb[3] if qg % 2 == 0 else s.pb[6]
                        for j in range(4):
                            tt_ = 4 * qg + j
                            v.wait_ge(s.s_ac.sem, ac_v + tt_ + 1)
                            if j > 0:
                                v.memset(s.A_A[:, tt_ * 512:
                                               tt_ * 512 + j * 128], 0.0)
                            v.tensor_mul(
                                s.A_A[:, tt_ * 512 + j * 128:
                                      tt_ * 512 + (j + 1) * 128],
                                s.A_A[:, tt_ * 512 + j * 128:
                                      tt_ * 512 + (j + 1) * 128],
                                s.tri[:])
                            s.dinc(v, s.s_dv)
                        pe_v += nkv + 1
                        v.wait_ge(s.s_pe.sem, pe_v)
                        v.reciprocal(out=s.rowl[0:1, :], in_=lb[0:1, :])
                        s.dinc(v, s.s_dv)
                        pe_v += 1
                        v.wait_ge(s.s_pe.sem, pe_v)
                        v.tensor_copy(out=s.A_S[:, 4096:4608],
                                      in_=s.pb[4][:, 0:512])
                        s.dinc(v, s.s_dv)
                        v.tensor_mul(
                            s.A_BIG[:, AT + qg * 512:AT + (qg + 1) * 512],
                            pvb[:], s.A_S[:, 4096:4608])
                        s.dinc(v, s.s_dv)
                        ac_v += nkv
                        dv_v += 7

                @blk.sync
                def _(sy, hl=hl, dv0=dv0):
                    sy.wait_ge(s.s_dv.sem, dv0 + 10 + _S2_QGS * 7)
                    s.s_st.inc(sy.dma_start(
                        out=t["at_in"].ap()[hl * 128:(hl + 1) * 128, :],
                        in_=s.A_BIG[:, AT:AT + 4096]), 16)

        if _S2_HEADS == 4:
            with nc.Block() as blk:
                @blk.gpsimd
                def _(g):
                    g.wait_ge(s.s_st.sem, s.s_st.val)
                    s.s_cc.inc(g.collective_compute(
                        "AllGather", ALU.bypass, replica_groups=GROUP4,
                        ins=[t["at_in"].ap().opt()],
                        outs=[t["at_g"].ap().opt()]))
            self.cc_at = s.s_cc.val
        self.dump("atg", [D, T], BF16, lambda: t["at_g"].ap())
        self.dump("E", [128, 16384], BF16, lambda: s.A_A[:, 0:16384])
        self.dump("rowl", [1, 512], F32, lambda: s.rowl[0:1, :])
        self.dump("linvb", [128, 512], F32, lambda: s.A_S[:, 4096:4608])
        self.dump("negcd", [128, 1], F32, lambda: s.negc[:, 0:1])
        self.dump("atin", [512, T], BF16, lambda: t["at_in"].ap())
        self.dump("vt", [128, 4096], BF16, lambda: s.A_BIG[:, 24576:28672])
        self.dump("vraw", [128, 4096], BF16, lambda: s.A_BIG[:, 8192:12288])
        self.dump("qraw", [128, 4096], BF16, lambda: s.A_BIG[:, 0:4096])

    def s3a_oproj(self):
        nc, t, s = self.nc, self.t, self
        SD = float(np.sqrt(float(D)))
        AC_T, DV_T = 6, 7
        with nc.Block() as blk:
            ld0, pe0, ac0, dv0 = s.s_ld.val, s.s_pe.val, s.s_ac.val, s.s_dv.val
            sl0 = [c.val for c in s.s_sl]
            ss0 = [c.val for c in s.s_ss]

            @blk.sync
            def _(sy):
                sy.wait_ge(s.s_cc.sem, self.cc_at)
                sy.reg_load(s.roff[3], s.offs[0:1, 48:49])
                src = bass.AP(t["at_g"], s.roff[3],
                              [[T, 128], [128 * T, 16], [1, TOK]])
                dst = bass.AP(s.A_A, 0, [[16384, 128], [TOK, 16], [1, TOK]])
                s.s_ld.inc(sy.dma_start(out=dst, in_=src, bounds_check=None),
                           16)
                for d in range(16):
                    grow = (d // 2) * 1024 + 3 * 256 + (d % 2) * 128
                    s.s_ld.inc(sy.dma_start(
                        out=s.A_BIG[:, d * 2048:(d + 1) * 2048],
                        in_=t["wat_g"].ap()[grow:grow + 128, :]), 16)
                for tt in range(NT):
                    if tt >= 2:
                        sy.wait_ge(s.s_dv.sem, dv0 + DV_T * (tt - 2) + 3)
                    s.s_sl[tt % 2].inc(sy.dma_start(
                        out=s.A_S[:, 2048:4096],
                        in_=t["x"].ap()[tt * 128:(tt + 1) * 128, :]), 16)
                    sy.wait_ge(s.s_dv.sem, dv0 + DV_T * tt + 3)
                    s.s_ss[tt % 2].inc(sy.dma_start(
                        out=t["x2_dram"].ap()[tt * 128:(tt + 1) * 128, :],
                        in_=s.A_S[:, 4096:6144]), 16)

            @blk.tensor
            def _(te):
                te.wait_ge(s.s_ld.sem, ld0 + 17 * 16)
                for tt in range(NT):
                    bs = 0 if tt % 2 == 0 else 4
                    if tt >= 2:
                        te.wait_ge(s.s_ac.sem, ac0 + AC_T * (tt - 2) + 1)
                    for fc in range(16):
                        for dg in range(4):
                            te.matmul(
                                s.pb[bs + dg][:],
                                s.A_A[:, fc * 1024 + tt * 128:
                                      fc * 1024 + (tt + 1) * 128],
                                s.A_BIG[:, fc * 2048 + dg * 512:
                                        fc * 2048 + (dg + 1) * 512],
                                start=(fc == 0), stop=False)
                    for dg in range(4):
                        te.matmul(s.pb[bs + dg][:], s.ones_b[0:1, 0:128],
                                  s.bo_bf[0:1, dg * 512:(dg + 1) * 512],
                                  start=False, stop=True)
                    s.dinc(te, s.s_pe)

            @blk.scalar
            def _(sc):
                for tt in range(NT):
                    bs = 0 if tt % 2 == 0 else 4
                    sc.wait_ge(s.s_pe.sem, pe0 + tt + 1)
                    if tt >= 1:
                        sc.wait_ge(s.s_dv.sem, dv0 + DV_T * (tt - 1) + 7)
                    for dg in range(4):
                        sc.activation(out=s.A_S[:, dg * 512:(dg + 1) * 512],
                                      in_=s.pb[bs + dg][:], func=AF.Copy)
                    s.dinc(sc, s.s_ac)
                    sc.wait_ge(s.s_dv.sem, dv0 + DV_T * tt + 1)
                    sc.activation(out=s.stc[:, 9:10], in_=s.stc[:, 7:8],
                                  func=AF.Sqrt, bias=s.eps[:, 0:1])
                    s.dinc(sc, s.s_ac)
                    sc.wait_ge(s.s_dv.sem, dv0 + DV_T * tt + 2)
                    sc.activation(out=s.A_S[:, 6144:8192],
                                  in_=s.A_S[:, 0:2048], func=AF.Identity,
                                  scale=s.stc[:, 3:4], bias=s.stc[:, 4:5])
                    s.dinc(sc, s.s_ac)
                    sc.wait_ge(s.s_dv.sem, dv0 + DV_T * tt + 4)
                    sc.activation(out=s.stc[:, 19:20], in_=s.stc[:, 17:18],
                                  func=AF.Sqrt, bias=s.eps[:, 0:1])
                    s.dinc(sc, s.s_ac)
                    sc.wait_ge(s.s_dv.sem, dv0 + DV_T * tt + 5)
                    sc.activation(out=s.A_S[:, 6144:8192],
                                  in_=s.A_S[:, 4096:6144], func=AF.Identity,
                                  scale=s.stc[:, 13:14], bias=s.stc[:, 14:15])
                    sc.drain()
                    sc.activation(out=s.stc[:, 21:22], in_=s.stc[:, 20:21],
                                  func=AF.Sqrt)
                    s.dinc(sc, s.s_ac)
                    sc.wait_ge(s.s_dv.sem, dv0 + DV_T * tt + 6)
                    sc.activation(out=s.A_S[:, 0:2048],
                                  in_=s.A_S[:, 6144:8192], func=AF.Identity,
                                  scale=s.stc[:, 27:28])
                    s.dinc(sc, s.s_ac)

            @blk.vector
            def _(v):
                def ln_stats(src, varcol):
                    v.reduce_sum(out=s.stc[:, 0:1], in_=src, axis=AX.X)
                    v.tensor_mul(s.A_S[:, 6144:8192], src, src)
                    v.drain()
                    v.reduce_sum(out=s.stc[:, 1:2], in_=s.A_S[:, 6144:8192],
                                 axis=AX.X)
                    v.tensor_scalar_mul(s.stc[:, 2:3], s.stc[:, 0:1], 1.0 / D)
                    v.drain()
                    v.tensor_scalar_mul(s.stc[:, 5:6], s.stc[:, 1:2], 1.0 / D)
                    v.tensor_mul(s.stc[:, 6:7], s.stc[:, 2:3], s.stc[:, 2:3])
                    v.drain()
                    v.tensor_sub(s.stc[:, varcol:varcol + 1], s.stc[:, 5:6],
                                 s.stc[:, 6:7])

                for tt in range(NT):
                    v.wait_ge(s.s_ac.sem, ac0 + AC_T * tt + 1)
                    ln_stats(s.A_S[:, 0:2048], 7)
                    s.dinc(v, s.s_dv)
                    v.wait_ge(s.s_ac.sem, ac0 + AC_T * tt + 2)
                    v.reciprocal(out=s.stc[:, 3:4], in_=s.stc[:, 9:10])
                    v.drain()
                    v.tensor_mul(s.stc[:, 8:9], s.stc[:, 2:3], s.stc[:, 3:4])
                    v.drain()
                    v.tensor_scalar_mul(s.stc[:, 4:5], s.stc[:, 8:9], -1.0)
                    s.dinc(v, s.s_dv)
                    v.wait_ge(s.s_ac.sem, ac0 + AC_T * tt + 3)
                    v.wait_ge(s.s_sl[tt % 2].sem,
                              sl0[tt % 2] + 16 * (tt // 2 + 1))
                    if tt >= 1:
                        v.wait_ge(s.s_ss[(tt - 1) % 2].sem,
                                  ss0[(tt - 1) % 2]
                                  + 16 * ((tt - 1) // 2 + 1))
                    v.tensor_add(s.A_S[:, 4096:6144], s.A_S[:, 6144:8192],
                                 s.A_S[:, 2048:4096])
                    s.dinc(v, s.s_dv)
                    ln_stats(s.A_S[:, 4096:6144], 17)
                    s.dinc(v, s.s_dv)
                    v.wait_ge(s.s_ac.sem, ac0 + AC_T * tt + 4)
                    v.reciprocal(out=s.stc[:, 13:14], in_=s.stc[:, 19:20])
                    v.drain()
                    v.tensor_mul(s.stc[:, 8:9], s.stc[:, 2:3],
                                 s.stc[:, 13:14])
                    v.tensor_mul(s.stc[:, 15:16], s.stc[:, 17:18],
                                 s.stc[:, 13:14])
                    v.drain()
                    v.tensor_scalar_mul(s.stc[:, 14:15], s.stc[:, 8:9], -1.0)
                    v.tensor_mul(s.stc[:, 16:17], s.stc[:, 15:16],
                                 s.stc[:, 13:14])
                    v.drain()
                    v.tensor_scalar_mul(s.stc[:, 20:21], s.stc[:, 16:17],
                                        float(D))
                    s.dinc(v, s.s_dv)
                    v.wait_ge(s.s_ac.sem, ac0 + AC_T * tt + 5)
                    v.reduce_max(out=s.stc[:, 22:23], in_=s.A_S[:, 6144:8192],
                                 axis=AX.X, apply_absolute_value=True)
                    v.tensor_scalar_max(s.stc[:, 23:24], s.stc[:, 21:22],
                                        1e-12)
                    v.drain()
                    v.reciprocal(out=s.stc[:, 24:25], in_=s.stc[:, 23:24])
                    v.drain()
                    v.tensor_scalar_mul(s.stc[:, 25:26], s.stc[:, 24:25], SD)
                    v.drain()
                    v.tensor_mul(s.stc[:, 26:27], s.stc[:, 22:23],
                                 s.stc[:, 25:26])
                    v.drain()
                    v.tensor_scalar_max(s.stc[:, 28:29], s.stc[:, 26:27],
                                        1e-5)
                    v.drain()
                    v.reciprocal(out=s.stc[:, 29:30], in_=s.stc[:, 28:29])
                    v.drain()
                    v.tensor_scalar_mul(s.stc[:, 30:31], s.stc[:, 29:30],
                                        127.0)
                    v.drain()
                    v.tensor_mul(s.stc[:, 27:28], s.stc[:, 25:26],
                                 s.stc[:, 30:31])
                    v.tensor_scalar_mul(s.stc[:, 31:32], s.stc[:, 28:29],
                                        1.0 / 127.0)
                    s.dinc(v, s.s_dv)
                    v.wait_ge(s.s_ac.sem, ac0 + AC_T * tt + 6)
                    v.tensor_copy(out=s.A_I[:], in_=s.A_S[:, 0:2048])
                    v.drain()
                    v.tensor_scalar(out=s.A_W[:, tt * 2048:(tt + 1) * 2048],
                                    in0=s.A_I[:], scalar1=s.stc[:, 31:32],
                                    scalar2=None, op0=ALU.mult)
                    s.dinc(v, s.s_dv)
        self.dump("x2", [TOK, D], F32, lambda: t["x2_dram"].ap())
        self.dump("xqi", [128, 16384], BF16, lambda: s.A_W[:, 0:16384])

    def s3b_ln23(self):
        nc, t, s = self.nc, self.t, self
        with nc.Block() as blk:
            pe0, dv0 = s.s_pe.val, s.s_dv.val

            @blk.tensor
            def _(te):
                for tt in range(NT):
                    for qb in range(4):
                        b = tt * 4 + qb
                        if b >= 2:
                            te.wait_ge(s.s_dv.sem, dv0 + b - 1)
                        for j in range(4):
                            dch = qb * 4 + j
                            te.matmul(
                                s.pb[4 + b % 2][:, j * 128:(j + 1) * 128],
                                s.A_W[:, tt * 2048 + dch * 128:
                                      tt * 2048 + (dch + 1) * 128],
                                s.ident[:], start=True, stop=True)
                        s.dinc(te, s.s_pe)

            @blk.vector
            def _(v):
                for tt in range(NT):
                    for qb in range(4):
                        b = tt * 4 + qb
                        v.wait_ge(s.s_pe.sem, pe0 + b + 1)
                        dst = bass.AP(s.A_A, qb * 4096 + tt * 128,
                                      [[16384, 128], [1024, 4], [1, 128]])
                        src = bass.AP(s.pb[4 + b % 2], 0,
                                      [[512, 128], [128, 4], [1, 128]])
                        v.tensor_copy(out=dst, in_=src)
                        s.dinc(v, s.s_dv)
        self.dump("xqT", [128, 16384], BF16, lambda: s.A_A[:, 0:16384])

    def s3c_ffn(self):
        nc, t, s = self.nc, self.t, self
        SDI = float(np.sqrt(float(INNER)))
        for quad in range(2):
            tts = [quad * 4 + i for i in range(4)]
            with nc.Block() as blk:
                pe0, ac0 = s.s_pe.val, s.s_ac.val
                sl0 = [c.val for c in s.s_sl]

                @blk.sync
                def _(sy, quad=quad):
                    if quad == 0:
                        sy.wait_ge(s.s_cc.sem, self.cc_wq)
                    for ig in range(16):
                        if ig >= 2:
                            sy.wait_ge(s.s_pe.sem, pe0 + 4 * (ig - 1))
                        for d in range(16):
                            s.s_sl[2 + ig % 2].inc(sy.dma_start(
                                out=s.A_W[:, (ig % 2) * 8192 + d * 512:
                                          (ig % 2) * 8192 + (d + 1) * 512],
                                in_=t["w1q_g"].ap()[d * 128:(d + 1) * 128,
                                                    ig * 512:(ig + 1) * 512]),
                                16)

                @blk.tensor
                def _(te, quad=quad):
                    for ig in range(16):
                        te.wait_ge(s.s_sl[2 + ig % 2].sem,
                                   sl0[2 + ig % 2] + 256 * (ig // 2 + 1))
                        for i, tt in enumerate(tts):
                            if ig >= 1:
                                te.wait_ge(s.s_ac.sem,
                                           ac0 + 4 * (ig - 1) + i + 1)
                            for d in range(16):
                                te.matmul(
                                    s.pb[i][:],
                                    s.A_A[:, (d * 8 + tt) * 128:
                                          (d * 8 + tt + 1) * 128],
                                    s.A_W[:, (ig % 2) * 8192 + d * 512:
                                          (ig % 2) * 8192 + (d + 1) * 512],
                                    start=(d == 0), stop=(d == 15))
                            s.dinc(te, s.s_pe)

                @blk.scalar
                def _(sc):
                    for ig in range(16):
                        for i in range(4):
                            sc.wait_ge(s.s_pe.sem, pe0 + 4 * ig + i + 1)
                            sc.activation(
                                out=s.A_S[:, (i % 2) * 512:(i % 2 + 1) * 512],
                                in_=s.pb[i][:], func=AF.Gelu,
                                scale=s.wscal[:, 2:3])
                            sc.drain()
                            sc.activation(
                                out=s.A_BIG[:, i * 8192 + ig * 512:
                                            i * 8192 + (ig + 1) * 512],
                                in_=s.A_S[:, (i % 2) * 512:(i % 2 + 1) * 512],
                                func=AF.Gelu)
                            s.dinc(sc, s.s_ac)

            with nc.Block() as blk:
                pe0, ac0, dv0 = s.s_pe.val, s.s_ac.val, s.s_dv.val
                ss_snap = [c.val for c in s.s_ss]
                AC_I, DV_I = 6, 22

                @blk.scalar
                def _(sc):
                    for j in range(2):
                        sc.wait_ge(s.s_ss[j].sem, ss_snap[j])
                    for i in range(4):
                        if i >= 1:
                            sc.wait_ge(s.s_dv.sem, dv0 + DV_I * (i - 1) + 6)
                        for c in range(4):
                            sc.activation(
                                out=s.A_S[:, c * 2048:(c + 1) * 2048],
                                in_=s.A_BIG[:, i * 8192 + c * 2048:
                                            i * 8192 + (c + 1) * 2048],
                                func=AF.Square,
                                accum_out=s.stc[:, 40 + c:41 + c])
                        s.dinc(sc, s.s_ac)
                        sc.wait_ge(s.s_dv.sem, dv0 + DV_I * i + 1)
                        sc.activation(out=s.stc[:, 46:47], in_=s.stc[:, 44:45],
                                      func=AF.Sqrt)
                        s.dinc(sc, s.s_ac)
                        for c in range(4):
                            sc.wait_ge(s.s_dv.sem, dv0 + DV_I * i + 2
                                       + (c if c > 0 else 0))
                            sc.activation(
                                out=s.A_S[:, 2048:4096],
                                in_=s.A_BIG[:, i * 8192 + c * 2048:
                                            i * 8192 + (c + 1) * 2048],
                                func=AF.Identity, scale=s.stc[:, 49:50])
                            s.dinc(sc, s.s_ac)

                @blk.vector
                def _(v):
                    for i in range(4):
                        v.wait_ge(s.s_ac.sem, ac0 + AC_I * i + 1)
                        v.tensor_add(s.stc[:, 50:51], s.stc[:, 40:41],
                                     s.stc[:, 41:42])
                        v.tensor_add(s.stc[:, 51:52], s.stc[:, 42:43],
                                     s.stc[:, 43:44])
                        v.drain()
                        v.tensor_add(s.stc[:, 44:45], s.stc[:, 50:51],
                                     s.stc[:, 51:52])
                        v.reduce_max(out=s.stc[:, 45:46],
                                     in_=s.A_BIG[:, i * 8192:(i + 1) * 8192],
                                     axis=AX.X, apply_absolute_value=True)
                        s.dinc(v, s.s_dv)
                        v.wait_ge(s.s_ac.sem, ac0 + AC_I * i + 2)
                        v.tensor_scalar_max(s.stc[:, 47:48], s.stc[:, 46:47],
                                            1e-12)
                        v.drain()
                        v.reciprocal(out=s.stc[:, 48:49], in_=s.stc[:, 47:48])
                        v.drain()
                        v.tensor_scalar_mul(s.stc[:, 47:48], s.stc[:, 48:49],
                                            SDI)
                        v.drain()
                        v.tensor_mul(s.stc[:, 52:53], s.stc[:, 45:46],
                                     s.stc[:, 47:48])
                        v.drain()
                        v.tensor_scalar_max(s.stc[:, 53:54], s.stc[:, 52:53],
                                            1e-5)
                        v.drain()
                        v.reciprocal(out=s.stc[:, 54:55], in_=s.stc[:, 53:54])
                        v.drain()
                        v.tensor_scalar_mul(s.stc[:, 55:56], s.stc[:, 54:55],
                                            127.0)
                        v.drain()
                        v.tensor_mul(s.stc[:, 49:50], s.stc[:, 47:48],
                                     s.stc[:, 55:56])
                        v.tensor_mul(s.stc[:, 56:57], s.stc[:, 53:54],
                                     s.wscal[:, 3:4])
                        v.drain()
                        v.tensor_scalar_mul(s.stc[:, 32 + tts[i]:
                                                  33 + tts[i]],
                                            s.stc[:, 56:57], 1.0 / 127.0)
                        s.dinc(v, s.s_dv)
                        for c in range(4):
                            v.wait_ge(s.s_ac.sem, ac0 + AC_I * i + 3 + c)
                            v.tensor_copy(out=s.A_I[:], in_=s.A_S[:, 2048:4096])
                            v.drain()
                            v.tensor_copy(
                                out=s.A_BIG[:, i * 8192 + c * 2048:
                                            i * 8192 + (c + 1) * 2048],
                                in_=s.A_I[:])
                            s.dinc(v, s.s_dv)
                        for cb in range(16):
                            b = i * 16 + cb
                            v.wait_ge(s.s_pe.sem, pe0 + b + 1)
                            dst = bass.AP(s.A_BIG, i * 8192 + cb * 512,
                                          [[32768, 128], [128, 4], [1, 128]])
                            src = bass.AP(s.pb[4 + b % 2], 0,
                                          [[512, 128], [128, 4], [1, 128]])
                            v.tensor_copy(out=dst, in_=src)
                            s.dinc(v, s.s_dv)

                @blk.tensor
                def _(te):
                    for i in range(4):
                        for cb in range(16):
                            b = i * 16 + cb
                            te.wait_ge(s.s_dv.sem,
                                       dv0 + DV_I * i + 3 + (cb // 4))
                            if b >= 2:
                                pb_, pc = divmod(b - 2, 16)
                                te.wait_ge(s.s_dv.sem,
                                           dv0 + DV_I * pb_ + 7 + pc)
                            for j in range(4):
                                ch = cb * 4 + j
                                te.matmul(
                                    s.pb[4 + b % 2][:, j * 128:(j + 1) * 128],
                                    s.A_BIG[:, i * 8192 + ch * 128:
                                            i * 8192 + (ch + 1) * 128],
                                    s.ident[:], start=True, stop=True)
                            s.dinc(te, s.s_pe)

            with nc.Block() as blk:
                pe0, ac0 = s.s_pe.val, s.s_ac.val
                sl0 = [c.val for c in s.s_sl]

                @blk.sync
                def _(sy):
                    for dg in range(4):
                        for ch in range(64):
                            k = dg * 64 + ch
                            if k >= 2:
                                sy.wait_ge(s.s_pe.sem, pe0 + k - 1)
                            s.s_sl[2 + k % 2].inc(sy.dma_start(
                                out=s.A_W[:, (k % 2) * 512:
                                          (k % 2 + 1) * 512],
                                in_=t["w2q_g"].ap()[ch * 128:(ch + 1) * 128,
                                                    dg * 512:(dg + 1) * 512]),
                                16)

                @blk.tensor
                def _(te):
                    for dg in range(4):
                        for ch in range(64):
                            k = dg * 64 + ch
                            te.wait_ge(s.s_sl[2 + k % 2].sem,
                                       sl0[2 + k % 2] + 16 * (k // 2 + 1))
                            if ch == 0 and dg > 0:
                                te.wait_ge(s.s_ac.sem, ac0 + 4 * dg)
                            for i in range(4):
                                te.matmul(
                                    s.pb[i][:],
                                    s.A_BIG[:, i * 8192 + ch * 128:
                                            i * 8192 + (ch + 1) * 128],
                                    s.A_W[:, (k % 2) * 512:(k % 2 + 1) * 512],
                                    start=(ch == 0), stop=(ch == 63))
                            s.dinc(te, s.s_pe)

                @blk.scalar
                def _(sc):
                    for dg in range(4):
                        sc.wait_ge(s.s_pe.sem, pe0 + 64 * (dg + 1))
                        for i in range(4):
                            sc.activation(
                                out=s.A_B[:, i * 2048 + dg * 512:
                                          i * 2048 + (dg + 1) * 512],
                                in_=s.pb[i][:], func=AF.Identity,
                                scale=s.stc[:, 32 + tts[i]:33 + tts[i]])
                            s.dinc(sc, s.s_ac)

            with nc.Block() as blk:
                ac0, dv0 = s.s_ac.val, s.s_dv.val
                sl0 = [c.val for c in s.s_sl]
                ss0 = [c.val for c in s.s_ss]

                @blk.sync
                def _(sy, quad=quad):
                    for i, tt in enumerate(tts):
                        if i >= 2:
                            sy.wait_ge(s.s_dv.sem, dv0 + 3 * (i - 2) + 3)
                        s.s_sl[i % 2].inc(sy.dma_start(
                            out=s.A_S[:, 2048:4096],
                            in_=t["x2_dram"].ap()[tt * 128:(tt + 1) * 128, :]),
                            16)
                        sy.wait_ge(s.s_dv.sem, dv0 + 3 * i + 3)
                        s.s_ss[i % 2].inc(sy.dma_start(
                            out=t["out"].ap()[tt * 128:(tt + 1) * 128, :],
                            in_=s.A_S[:, 4096:6144]), 16)

                @blk.vector
                def _(v):
                    def ln_stats(src, varcol):
                        v.reduce_sum(out=s.stc[:, 0:1], in_=src, axis=AX.X)
                        v.tensor_mul(s.A_S[:, 6144:8192], src, src)
                        v.drain()
                        v.reduce_sum(out=s.stc[:, 1:2],
                                     in_=s.A_S[:, 6144:8192], axis=AX.X)
                        v.tensor_scalar_mul(s.stc[:, 2:3], s.stc[:, 0:1],
                                            1.0 / D)
                        v.drain()
                        v.tensor_scalar_mul(s.stc[:, 5:6], s.stc[:, 1:2],
                                            1.0 / D)
                        v.tensor_mul(s.stc[:, 6:7], s.stc[:, 2:3],
                                     s.stc[:, 2:3])
                        v.drain()
                        v.tensor_sub(s.stc[:, varcol:varcol + 1],
                                     s.stc[:, 5:6], s.stc[:, 6:7])

                    for i in range(4):
                        fsrc = s.A_B[:, i * 2048:(i + 1) * 2048]
                        ln_stats(fsrc, 7)
                        s.dinc(v, s.s_dv)
                        v.wait_ge(s.s_ac.sem, ac0 + 2 * i + 1)
                        v.reciprocal(out=s.stc[:, 3:4], in_=s.stc[:, 9:10])
                        v.drain()
                        v.tensor_mul(s.stc[:, 8:9], s.stc[:, 2:3],
                                     s.stc[:, 3:4])
                        v.drain()
                        v.tensor_scalar_mul(s.stc[:, 4:5], s.stc[:, 8:9],
                                            -1.0)
                        s.dinc(v, s.s_dv)
                        v.wait_ge(s.s_ac.sem, ac0 + 2 * i + 2)
                        v.wait_ge(s.s_sl[i % 2].sem,
                                  sl0[i % 2] + 16 * (i // 2 + 1))
                        if i >= 1:
                            v.wait_ge(s.s_ss[(i - 1) % 2].sem,
                                      ss0[(i - 1) % 2]
                                      + 16 * ((i - 1) // 2 + 1))
                        v.tensor_add(s.A_S[:, 4096:6144], s.A_S[:, 0:2048],
                                     s.A_S[:, 2048:4096])
                        s.dinc(v, s.s_dv)

                @blk.scalar
                def _(sc):
                    for i in range(4):
                        sc.wait_ge(s.s_dv.sem, dv0 + 3 * i + 1)
                        sc.activation(out=s.stc[:, 9:10], in_=s.stc[:, 7:8],
                                      func=AF.Sqrt, bias=s.eps[:, 0:1])
                        s.dinc(sc, s.s_ac)
                        sc.wait_ge(s.s_dv.sem, dv0 + 3 * i + 2)
                        sc.activation(out=s.A_S[:, 0:2048],
                                      in_=s.A_B[:, i * 2048:(i + 1) * 2048],
                                      func=AF.Identity, scale=s.stc[:, 3:4],
                                      bias=s.stc[:, 4:5])
                        s.dinc(sc, s.s_ac)

    def epilogue(self):
        nc, s = self.nc, self
        with nc.Block() as blk:
            @blk.gpsimd
            def _(g):
                g.wait_ge(s.s_st.sem, s.s_st.val)
                g.wait_ge(s.s_cc.sem, s.s_cc.val)
                for i in range(4):
                    g.wait_ge(s.s_ss[i].sem, s.s_ss[i].val)
                    g.wait_ge(s.s_sl[i].sem, s.s_sl[i].val)
                g.wait_ge(s.s_ld.sem, s.s_ld.val)


def stage_inputs(inputs):
    x = np.ascontiguousarray(inputs["L0_attn_query"])
    w_q = inputs["L0_attn_w_q"]; w_k = inputs["L0_attn_w_k"]
    w_v = inputs["L0_attn_w_v"]; w_o = inputs["L0_attn_w_o"]
    b_q = inputs["L0_attn_b_q"]; b_k = inputs["L0_attn_b_k"]
    b_v = inputs["L0_attn_b_v"]; b_o = inputs["L0_attn_b_o"]
    w1t = np.ascontiguousarray(np.asarray(inputs["L0_ffn_w1"]).T)
    w2t = np.ascontiguousarray(np.asarray(inputs["L0_ffn_w2"]).T)

    bias_pp = np.empty((128, 48), np.float32)
    for j in range(16):
        bias_pp[:, j] = b_q[j * 128:(j + 1) * 128]
        bias_pp[:, 16 + j] = b_k[j * 128:(j + 1) * 128]
        bias_pp[:, 32 + j] = b_v[j * 128:(j + 1) * 128]
    bo_row = np.asarray(b_o, np.float32).reshape(1, D)
    ident = np.eye(128, dtype=ml_dtypes.bfloat16)
    tri = np.tril(np.ones((128, 128), np.float32)).T
    tri = np.ascontiguousarray(np.tril(np.ones((128, 128), np.float32)).T
                               ).astype(ml_dtypes.bfloat16)

    maps = []
    for c in range(8):
        b, cg = c // 4, c % 4
        hoffs = np.zeros((1, 80), np.int32)
        for tl in range(4):
            hg = 4 * cg + tl
            for p in range(3):
                row = hg * 384 + p * 128
                hoffs[0, tl * 3 + p] = row * TOK
        hoffs[0, 48] = cg * TOK
        m = dict(
            x=np.ascontiguousarray(x[b, cg * TOK:(cg + 1) * TOK, :]),
            wq_sh=np.ascontiguousarray(w_q[c * 256:(c + 1) * 256, :]),
            wk_sh=np.ascontiguousarray(w_k[c * 256:(c + 1) * 256, :]),
            wv_sh=np.ascontiguousarray(w_v[c * 256:(c + 1) * 256, :]),
            wo_sh=np.ascontiguousarray(w_o[c * 256:(c + 1) * 256, :]),
            w1t_sh=np.ascontiguousarray(w1t[c * 256:(c + 1) * 256, :]),
            w2t_sh=np.ascontiguousarray(w2t[c * 1024:(c + 1) * 1024, :]),
            bias_pp=bias_pp, bo_row=bo_row, ident=ident, tri=tri,
            hoffs=hoffs,
        )
        maps.append({k: np.ascontiguousarray(v) for k, v in m.items()})
    return maps


def kernel(**inputs):
    maps = stage_inputs(inputs)
    bld = Builder()
    nc = bld.build()
    res = run_bass_kernel_spmd(nc, maps, core_ids=list(range(8)))
    out = np.empty((B, T, D), np.float32)
    for c in range(8):
        b, cg = c // 4, c % 4
        out[b, cg * TOK:(cg + 1) * TOK, :] = res.results[c]["out"]
    if _DEBUG:
        return out, [
            {k: r[k] for k in r if k.startswith("dbg_")} for r in res.results]
    return out
